# version 38
# baseline (speedup 1.0000x reference)
import numpy as np
from contextlib import ExitStack

import concourse.bass as bass
import concourse.mybir as mybir
from concourse.bass_utils import run_bass_kernel_spmd

F32 = mybir.dt.float32
BF16 = mybir.dt.bfloat16
AF = mybir.ActivationFunctionType
ALU = mybir.AluOpType

D = 1024
T = 4096
H = 4096
WID = 2048
NTREE = 8
NNODE = 255
DC = D // 128
TC = T // 512
TT = T // 128
HC = H // 128
WC = WID // 512
WK = WID // 128

MAGIC = 12582912.0
INV2PI = float(np.float32(1.0 / (2.0 * np.pi)))
CW1 = 6.28125
CW2 = float(np.float32(2.0 * np.pi - 6.28125))
PI = float(np.float32(np.pi))
HALFPI = float(np.float32(np.pi / 2.0))
TWOPI = float(np.float32(2.0 * np.pi))
LN_EPS = 1e-5


class Prog:

    ENGS = ("pe", "act", "dve", "sp", "gp")

    def __init__(self, nc, ctx):
        self.nc = nc
        self.ctx = ctx
        self.sem = {e: ctx.enter_context(nc.semaphore("sem_" + e)) for e in self.ENGS}
        self.cnt = {e: 0 for e in self.ENGS}
        self.wm = {e: {} for e in self.ENGS}
        self.q = {e: [] for e in self.ENGS}
        self.bankrel = {}

    def emit(self, eng, fn, deps=(), inc=True, dma=False, sem=None):
        key = sem if sem is not None else eng
        if key not in self.sem:
            self.sem[key] = self.ctx.enter_context(
                self.nc.semaphore("sem_" + key))
            self.cnt[key] = 0
        waits = []
        for d in deps:
            if not d:
                continue
            s, v = d
            if v > self.wm[eng].get(s, 0):
                waits.append((s, v))
                self.wm[eng][s] = v
        step = 16 if dma else 1
        tok = None
        if inc:
            self.cnt[key] += step
            tok = (key, self.cnt[key])
        semmap = self.sem
        myname = key

        def closure(engine_obj, waits=tuple(waits), fn=fn, tok=tok, step=step):
            for (s, v) in waits:
                engine_obj.wait_ge(semmap[s], v)
            inst = fn(engine_obj)
            if tok is not None:
                inst.then_inc(semmap[myname], step)

        self.q[eng].append(closure)
        return tok

    def last(self, eng):
        return (eng, self.cnt[eng]) if self.cnt.get(eng, 0) > 0 else None

    def bank_take(self, bid):
        return self.bankrel.pop(bid, [])

    def bank_read(self, bid, tok):
        self.bankrel.setdefault(bid, []).append(tok)
        return tok

    def barrier(self):
        toks = [(k, self.cnt[k]) for k in self.cnt if self.cnt[k] > 0]
        semmap = self.sem
        for e in self.ENGS:
            def closure(engine_obj, toks=tuple(toks), e=e):
                for (s, v) in toks:
                    engine_obj.wait_ge(semmap[s], v)
            self.q[e].append(closure)
            for (s, v) in toks:
                if v > self.wm[e].get(s, 0):
                    self.wm[e][s] = v
        self.bankrel = {}
        return toks

    def flush(self):
        nc = self.nc
        qs = {e: self.q[e] for e in self.ENGS}
        self.q = {e: [] for e in self.ENGS}
        if not any(qs.values()):
            return
        with nc.Block() as block:
            @block.tensor
            def _(e, fns=qs["pe"]):
                for f in fns:
                    f(e)

            @block.scalar
            def _(e, fns=qs["act"]):
                for f in fns:
                    f(e)

            @block.vector
            def _(e, fns=qs["dve"]):
                for f in fns:
                    f(e)

            @block.sync
            def _(e, fns=qs["sp"]):
                for f in fns:
                    f(e)

            @block.gpsimd
            def _(e, fns=qs["gp"]):
                for f in fns:
                    f(e)


def hap(t, off, dims):
    return bass.AP(t, off, [list(d) for d in dims])


P7CUT = 99


def build_program(debug=False, upto=99):
    nc = bass.Bass("TRN2", target_bir_lowering=False)

    t_xT = nc.dram_tensor("xT", [128, TC * DC * 512], F32, kind="ExternalInput")
    t_W0 = nc.dram_tensor("W0", [D, D], F32, kind="ExternalInput")
    t_b0v = nc.dram_tensor("b0v", [128, DC], F32, kind="ExternalInput")
    t_w1n = nc.dram_tensor("w1n", [H, D], F32, kind="ExternalInput")
    t_w1t = nc.dram_tensor("w1t", [D, H], F32, kind="ExternalInput")
    t_b1v = nc.dram_tensor("b1v", [128, HC], F32, kind="ExternalInput")
    t_w2n = nc.dram_tensor("w2n", [D, H], F32, kind="ExternalInput")
    t_w2t = nc.dram_tensor("w2t", [H, D], F32, kind="ExternalInput")
    t_b2v = nc.dram_tensor("b2v", [128, DC], F32, kind="ExternalInput")
    t_WvT = nc.dram_tensor("WvT", [D, D], F32, kind="ExternalInput")
    t_WoT = nc.dram_tensor("WoT", [D, D], F32, kind="ExternalInput")
    t_freqv = nc.dram_tensor("freqv", [128, DC], F32, kind="ExternalInput")
    t_WinT = nc.dram_tensor("WinT", [D, WID], F32, kind="ExternalInput")
    t_binr = nc.dram_tensor("binr", [1, WID], F32, kind="ExternalInput")
    t_WoutT = nc.dram_tensor("WoutT", [WID, D], F32, kind="ExternalInput")
    t_out = nc.dram_tensor("out", [D, T], F32, kind="ExternalOutput")

    dbg = "ExternalOutput" if debug else "Internal"
    t_xnsp = nc.dram_tensor("xnsp", [128, TC * DC * 512], BF16, kind=dbg)
    t_pT = nc.dram_tensor("pT", [D, T], BF16, kind=dbg)
    t_x1T = nc.dram_tensor("x1T", [128, TC * DC * 512], F32, kind=dbg)
    t_p1T = nc.dram_tensor("p1T", [D, T], BF16, kind=dbg)

    with ExitStack() as ctx:
        p = Prog(nc, ctx)

        _sbn = [0]

        def sb(name, shape, dt, st=None, side=None):
            _sbn[0] += 1
            kw = {"side": side} if side else {}
            return (st or ctx).enter_context(
                nc.sbuf_tensor(f"sb{_sbn[0]}_{name}", shape, dt, **kw))

        pb = [ctx.enter_context(nc.psum_tensor(f"pb{i}", [128, 512], F32))
              for i in range(8)]

        ones_col = sb("ones_col", [128, 1], BF16)
        ones_row = sb("ones_row", [1, 128], BF16)
        ones_rf = sb("ones_rf", [1, 128], F32)
        b0v = sb("b0v", [128, DC], F32)
        b1v = sb("b1v", [128, HC], F32)
        b2v = sb("b2v", [128, DC], F32)
        freqv = sb("freqv", [128, DC], F32)
        alpha1 = sb("alpha1", [128, HC], F32)
        alpha2 = sb("alpha2", [128, DC], F32)

        p.emit("dve", lambda e: e.memset(ones_col[:, :], 1.0))
        p.emit("dve", lambda e: e.memset(ones_row[:, :], 1.0))
        p.emit("dve", lambda e: e.memset(ones_rf[:, :], 1.0))
        p.emit("sp", lambda e: e.dma_start(out=b0v[:, :], in_=t_b0v.ap()), dma=True)
        p.emit("sp", lambda e: e.dma_start(out=b1v[:, :], in_=t_b1v.ap()), dma=True)
        p.emit("sp", lambda e: e.dma_start(out=b2v[:, :], in_=t_b2v.ap()), dma=True)
        p.emit("sp", lambda e: e.dma_start(out=freqv[:, :], in_=t_freqv.ap()), dma=True)
        p.barrier()

        def ln_pass(src_dram, sink_dram=None, sink_sbuf=None, units=None):
            units = list(units or [])
            nsteps = TC + 2
            nun = max(1, (len(units) + nsteps - 1) // nsteps) if units else 0
            with ExitStack() as st:
                xc = [sb(f"ln_xc{i}", [128, DC * 512], F32, st) for i in range(4)]
                xcb = [sb(f"ln_xcb{i}", [128, T], BF16, st) for i in range(2)]
                xsq = [sb("ln_xsq0", [128, T], BF16, st)] * 2
                scr = sb("ln_scr", [128, 512], F32, st)
                mrow = [sb(f"ln_mrow{i}", [1, 512], F32, st) for i in range(2)]
                rrow = [sb(f"ln_rrow{i}", [1, 512], F32, st) for i in range(2)]
                s3 = [sb(f"ln_s3{i}", [1, 512], F32, st) for i in range(2)]
                s4 = [sb(f"ln_s4{i}", [1, 512], F32, st) for i in range(2)]
                xnc = [sb("ln_xnc0", [128, DC * 512], BF16, st)] * 2
                ld_tok = [None] * 4
                xc_rel = [None] * 4
                scr_rel = [None]
                xnc_rel = [None]
                t12_tok = {}
                mm_last = {}
                t5_tok = {}
                t7_tok = {}
                t89_tok = {}
                SBANKS = [(4, 5), (2, 3)]
                BBANKS = [(6, 7), (0, 1)]
                ldeng = "gp" if src_dram.dtype == BF16 else "sp"

                def emit_ld(tc):
                    i3 = tc % 4
                    src = src_dram.ap()[:, tc * DC * 512:(tc + 1) * DC * 512]
                    p.emit(
                        ldeng, lambda e, o=xc[i3], s=src: e.dma_start(out=o[:, :], in_=s),
                        deps=[xc_rel[i3]], dma=True, sem=f"lnx{i3}")
                    ld_tok[i3] = p.last(f"lnx{i3}")

                def stage_a(tc):
                    i = tc % 2
                    i3 = tc % 4
                    sb0, sb1 = SBANKS[i]
                    ct = p.emit("act", lambda e, i=i, i3=i3: e.activation(
                        out=xcb[i][:, :], in_=xc[i3][:, :], func=AF.Identity),
                        deps=[ld_tok[i3], mm_last.get(tc - 2)])
                    sq = p.emit("dve", lambda e, i=i: e.tensor_tensor(
                        xsq[i][:, :], xcb[i][:, :], xcb[i][:, :], ALU.mult),
                        deps=[ct, mm_last.get(tc - 1)])
                    d4 = p.bank_take(sb0)
                    d5 = p.bank_take(sb1)
                    for c in range(DC):
                        p.emit("pe", lambda e, c=c, i=i, sb0=sb0: e.matmul(
                            pb[sb0][0:1, :], ones_col[:, :],
                            xcb[i][:, c * 512:(c + 1) * 512],
                            start=(c == 0), stop=(c == DC - 1)),
                            deps=[ct] + d4, inc=False)
                    mm_tok = None
                    for c in range(DC):
                        mm_tok = p.emit("pe", lambda e, c=c, i=i, sb1=sb1: e.matmul(
                            pb[sb1][0:1, :], ones_col[:, :],
                            xsq[i][:, c * 512:(c + 1) * 512],
                            start=(c == 0), stop=(c == DC - 1)),
                            deps=[sq] + d5, inc=(c == DC - 1))
                    mm_last[tc] = mm_tok
                    t89p = t89_tok.get(tc - 2, (None, None))
                    t1 = p.bank_read(sb0, p.emit("dve", lambda e, i=i, sb0=sb0: e.tensor_single_scalar(
                        mrow[i][0:1, :], pb[sb0][0:1, :], 1.0 / D, ALU.mult),
                        deps=[mm_tok, t7_tok.get(tc - 2)]))
                    t2 = p.bank_read(sb1, p.emit("dve", lambda e, i=i, sb1=sb1: e.tensor_single_scalar(
                        s3[i][0:1, :], pb[sb1][0:1, :], 1.0 / D, ALU.mult),
                        deps=[mm_tok, t89p[1]]))
                    t12_tok[tc] = (t1, t2)

                def stage_b(tc):
                    i = tc % 2
                    bb0, bb1 = BBANKS[i]
                    t1, t2 = t12_tok.pop(tc)
                    t89p = t89_tok.get(tc - 2, (None, None))
                    t3 = p.emit("dve", lambda e, i=i: e.tensor_tensor(
                        s4[i][0:1, :], mrow[i][0:1, :], mrow[i][0:1, :], ALU.mult),
                        deps=[t1, t5_tok.get(tc - 2)])
                    t4 = p.emit("dve", lambda e, i=i: e.tensor_tensor(
                        s3[i][0:1, :], s3[i][0:1, :], s4[i][0:1, :], ALU.subtract),
                        deps=[t2, t3])
                    t4b = p.emit("dve", lambda e, i=i: e.tensor_single_scalar(
                        s4[i][0:1, :], s3[i][0:1, :], LN_EPS, ALU.add), deps=[t4])
                    t5 = p.emit("act", lambda e, i=i: e.activation(
                        out=s3[i][0:1, :], in_=s4[i][0:1, :], func=AF.Ln), deps=[t4b])
                    t6 = p.emit("act", lambda e, i=i: e.activation(
                        out=rrow[i][0:1, :], in_=s3[i][0:1, :], func=AF.Exp,
                        scale=-0.5), deps=[t5, t89p[0]])
                    t7 = p.emit("dve", lambda e, i=i: e.scalar_tensor_tensor(
                        s3[i][0:1, :], mrow[i][0:1, :], -1.0, rrow[i][0:1, :],
                        ALU.mult, ALU.mult), deps=[t6])
                    d6 = p.bank_take(bb0)
                    d7 = p.bank_take(bb1)
                    t8 = p.emit("pe", lambda e, i=i, bb0=bb0: e.matmul(
                        pb[bb0][:, :], ones_rf[0:1, :], rrow[i][0:1, :],
                        start=True, stop=True), deps=[t6] + d6)
                    t9 = p.emit("pe", lambda e, i=i, bb1=bb1: e.matmul(
                        pb[bb1][:, :], ones_rf[0:1, :], s3[i][0:1, :],
                        start=True, stop=True), deps=[t7] + d7)
                    t5_tok[tc] = t5
                    t7_tok[tc] = t7
                    t89_tok[tc] = (t8, t9)

                def stage_c(tc):
                    i = tc % 2
                    i3 = tc % 4
                    bb0, bb1 = BBANKS[i]
                    t8, t9 = t89_tok[tc]
                    last = scr_rel[0]
                    for c in range(DC):
                        a1 = p.emit("dve", lambda e, i3=i3, c=c, bb0=bb0: e.tensor_tensor(
                            scr[:, :], xc[i3][:, c * 512:(c + 1) * 512], pb[bb0][:, :],
                            ALU.mult), deps=[t8, ld_tok[i3], xnc_rel[0], last])
                        if sink_sbuf is not None:
                            out_ap = sink_sbuf[:, c * T + tc * 512: c * T + (tc + 1) * 512]
                            a2 = p.emit("dve", lambda e, o=out_ap, bb1=bb1: e.tensor_tensor(
                                o, scr[:, :], pb[bb1][:, :], ALU.add), deps=[t9, a1])
                        else:
                            a2 = p.emit("dve", lambda e, i=i, c=c, bb1=bb1: e.tensor_tensor(
                                xnc[i][:, c * 512:(c + 1) * 512], scr[:, :],
                                pb[bb1][:, :], ALU.add), deps=[t9, a1])
                        last = a2
                    p.bank_read(bb0, last)
                    p.bank_read(bb1, last)
                    scr_rel[0] = last
                    xc_rel[i3] = last
                    if sink_dram is not None:
                        dst = sink_dram.ap()[:, tc * DC * 512:(tc + 1) * DC * 512]
                        xnc_rel[0] = p.emit("sp", lambda e, i=i, d=dst: e.dma_start(
                            out=d, in_=xnc[i][:, :]), deps=[last], dma=True,
                            sem="lns")

                for k in range(nsteps):
                    if k == 0:
                        emit_ld(0)
                    if k + 1 < TC:
                        emit_ld(k + 1)
                    if k < TC:
                        stage_a(k)
                    if 1 <= k <= TC:
                        stage_b(k - 1)
                    if k >= 2:
                        stage_c(k - 2)
                    for _ in range(nun):
                        if units:
                            units.pop(0)()
                while units:
                    units.pop(0)()
                p.barrier()
                p.flush()

        s12 = ctx.enter_context(ExitStack())
        s35 = None
        s68 = None

        if upto >= 1:
            sw1T = sb("sw1T", [128, DC * H], BF16, s12)
            p1_stack = ctx.enter_context(ExitStack())
            w1c = [sb(f"w1c{i}", [128, D], F32, p1_stack) for i in range(2)]
            w1tc = [sb(f"w1tc{i}", [128, 2048], F32, p1_stack) for i in range(2)]
            a1raw = sb("a1raw", [128, HC], F32, p1_stack)
            p1_state = {"w1rel": [None, None], "sg1rel": [None, None], "red1": []}

            def w1a_load(hc, st=p1_state):
                i = hc % 2
                src = t_w1n.ap()[hc * 128:(hc + 1) * 128, :]
                p.emit("gp", lambda e, i=i, s=src: e.dma_start(
                    out=w1c[i][:, :], in_=s), deps=[st["w1rel"][0]], dma=True,
                    sem=f"w1a{i}")

            def w1_alpha(hc, st=p1_state):
                if hc == 0:
                    w1a_load(0)
                if hc + 1 < HC:
                    w1a_load(hc + 1)
                i = hc % 2
                rd = p.emit("dve", lambda e, i=i, hc=hc: e.reduce_sum(
                    a1raw[:, hc:hc + 1], w1c[i][:, :], mybir.AxisListType.X,
                    apply_absolute_value=True), deps=[p.last(f"w1a{i}")])
                st["w1rel"][0] = rd
                st["red1"].append(rd)

            def w1g_load(k, st=p1_state):
                c, half = k // 2, k % 2
                i = k % 2
                src = t_w1t.ap()[c * 128:(c + 1) * 128,
                                 half * 2048:(half + 1) * 2048]
                p.emit("gp", lambda e, i=i, s=src: e.dma_start(
                    out=w1tc[i][:, :], in_=s), deps=[st["sg1rel"][0]], dma=True,
                    sem=f"w1g{i}")

            def w1_sign(k, st=p1_state):
                if k == 0:
                    w1g_load(0)
                if k + 1 < 2 * DC:
                    w1g_load(k + 1)
                c, half = k // 2, k % 2
                i = k % 2
                sg = p.emit("act", lambda e, i=i, c=c, half=half: e.activation(
                    out=sw1T[:, c * H + half * 2048: c * H + (half + 1) * 2048],
                    in_=w1tc[i][:, :], func=AF.Sign), deps=[p.last(f"w1g{i}")])
                st["sg1rel"][0] = sg
                st["sgtok", k] = sg

            def alpha1_scale(st=p1_state):
                p.emit("dve", lambda e: e.tensor_single_scalar(
                    alpha1[:, :], a1raw[:, :], 1.0 / D, ALU.mult),
                    deps=st["red1"][-1:])

            p1_units = []
            for hc in range(HC):
                p1_units.append(lambda hc=hc: w1_alpha(hc))
                if hc % 2 == 0:
                    p1_units.append(lambda k=hc // 2: w1_sign(k))
            p1_units.append(alpha1_scale)
        else:
            p1_units = []

        if upto >= 0:
            ln_pass(t_xT, sink_dram=t_xnsp, units=p1_units if upto >= 1 else None)
        elif upto >= 1:
            for u in p1_units:
                u()
            p.barrier()
            p.flush()
        if upto >= 1:
            p1_stack.close()

        w2sg_last = None
        alpha2_tok = None
        if upto >= 1:
            sw2T = sb("sw2T", [128, HC * D], BF16, s12)
            w2c = [sb(f"w2c{i}", [128, H], BF16, s12) for i in range(2)]
            w2tc = [sb(f"w2tc{i}", [128, D], F32, s12) for i in range(2)]
            a2raw = sb("a2raw", [128, DC], F32, s12)
            w2rel = [None, None]
            sg2rel = [None, None]
            red2 = None
            def w2a_load(dc):
                i = dc % 2
                src = t_w2n.ap()[dc * 128:(dc + 1) * 128, :]
                p.emit("gp", lambda e, i=i, s=src: e.dma_start(
                    out=w2c[i][:, :], in_=s), deps=[w2rel[0]], dma=True,
                    sem=f"w2a{i}")

            w2a_load(0)
            for dc in range(DC):
                if dc + 1 < DC:
                    w2a_load(dc + 1)
                i = dc % 2
                red2 = p.emit("dve", lambda e, i=i, dc=dc: e.reduce_sum(
                    a2raw[:, dc:dc + 1], w2c[i][:, :], mybir.AxisListType.X,
                    apply_absolute_value=True), deps=[p.last(f"w2a{i}")])
                w2rel[0] = red2
            alpha2_tok = p.emit("dve", lambda e: e.tensor_single_scalar(
                alpha2[:, :], a2raw[:, :], 1.0 / H, ALU.mult), deps=[red2])

            def w2g_load(hc):
                i = hc % 2
                src = t_w2t.ap()[hc * 128:(hc + 1) * 128, :]
                p.emit("sp", lambda e, i=i, s=src: e.dma_start(
                    out=w2tc[i][:, :], in_=s), deps=[sg2rel[0]], dma=True,
                    sem=f"w2g{i}")

            def w2_sign(hc):
                if hc == 0:
                    w2g_load(0)
                if hc + 1 < HC:
                    w2g_load(hc + 1)
                i = hc % 2
                sg = p.emit("act", lambda e, i=i, hc=hc: e.activation(
                    out=sw2T[:, hc * D:(hc + 1) * D],
                    in_=w2tc[i][:, :], func=AF.Sign), deps=[p.last(f"w2g{i}")])
                sg2rel[0] = sg

            w2units = [lambda hc=hc: w2_sign(hc) for hc in range(HC)]

        if upto >= 2:
            with ExitStack() as st:
                xnck = [sb(f"f_xn{i}", [128, DC * 512], BF16, st) for i in range(2)]
                gTc = sb("gTc", [128, HC * 512], BF16, st)
                xrs = [sb(f"xrs{i}", [128, 512], F32, st) for i in range(2)]
                sc1 = sb("sc1", [128, 512], BF16, st)
                gfl = sb("gfl", [1, 1], BF16, st)
                pout = [sb("pout0", [128, 512], BF16, st)] * 2
                xn_rel = [None, None]
                xr_rel = [p.last("w2a0"), p.last("w2a1")]
                po_rel = [None, None]
                prev_B_last = None
                for tc in range(TC):
                    i = tc % 2
                    src = t_xnsp.ap()[:, tc * DC * 512:(tc + 1) * DC * 512]
                    p.emit("sp", lambda e, i=i, s=src: e.dma_start(
                        out=xnck[i][:, :], in_=s), deps=[xn_rel[i]], dma=True,
                        sem=f"p2x{i}")
                    ld = p.last(f"p2x{i}")
                    gel_tok = {}
                    for h in range(HC):
                        bank = h % 4
                        dd = p.bank_take(bank)
                        mmt = None
                        for c in range(DC):
                            mmt = p.emit("pe", lambda e, h=h, c=c, i=i, bank=bank: e.matmul(
                                pb[bank][:, :],
                                sw1T[:, c * H + h * 128: c * H + (h + 1) * 128],
                                xnck[i][:, c * 512:(c + 1) * 512],
                                start=(c == 0), stop=(c == DC - 1)),
                                deps=[ld] + dd, inc=(c == DC - 1))
                        gel_tok[h] = p.bank_read(bank, p.emit(
                            "act", lambda e, h=h, bank=bank: e.activation(
                                out=gTc[:, h * 512:(h + 1) * 512], in_=pb[bank][:, :],
                                func=AF.Gelu, scale=alpha1[:, h:h + 1],
                                bias=b1v[:, h:h + 1]),
                            deps=[mmt, prev_B_last]))
                        if w2units:
                            w2units.pop(0)()
                    xn_rel[i] = gel_tok[HC - 1]
                    gflush = p.emit("act", lambda e: e.activation(
                        out=gfl[0:1, 0:1], in_=gTc[0:1, 0:1], func=AF.Identity),
                        deps=[gel_tok[HC - 1]])
                    for half in range(2):
                        for dq4 in range(4):
                            dout = half * 4 + dq4
                            bank = 4 + dq4
                            dd = p.bank_take(bank)
                            mmt = None
                            for h in range(HC):
                                mmt = p.emit("pe", lambda e, h=h, dout=dout, bank=bank: e.matmul(
                                    pb[bank][:, :],
                                    sw2T[:, h * D + dout * 128: h * D + (dout + 1) * 128],
                                    gTc[:, h * 512:(h + 1) * 512],
                                    start=(h == 0), stop=(h == HC - 1)),
                                    deps=[gel_tok[h], gflush, sg2rel[0], sg2rel[1]]
                                    + (dd if h == 0 else []),
                                    inc=(h == HC - 1))
                            j = dout % 2
                            xsrc = t_xT.ap()[:, tc * DC * 512 + dout * 512:
                                             tc * DC * 512 + (dout + 1) * 512]
                            p.emit("gp", lambda e, j=j, s=xsrc: e.dma_start(
                                out=xrs[j][:, :], in_=s), deps=[xr_rel[j]],
                                dma=True, sem=f"p2r{j}")
                            lt = p.last(f"p2r{j}")
                            et = p.bank_read(bank, p.emit(
                                "act", lambda e, dout=dout, bank=bank: e.activation(
                                    out=sc1[:, :], in_=pb[bank][:, :], func=AF.Identity,
                                    scale=alpha2[:, dout:dout + 1],
                                    bias=b2v[:, dout:dout + 1]),
                                deps=[mmt, prev_B_last, alpha2_tok]))
                            at = p.emit("dve", lambda e, j=j: e.tensor_tensor(
                                pout[j][:, :], sc1[:, :], xrs[j][:, :], ALU.add),
                                deps=[et, lt, po_rel[0], po_rel[1]])
                            prev_B_last = at
                            xr_rel[j] = at
                            dstp = t_pT.ap()[dout * 128:(dout + 1) * 128,
                                             tc * 512:(tc + 1) * 512]
                            po_rel[j] = p.emit("sp", lambda e, j=j, d=dstp: e.dma_start(
                                out=d, in_=pout[j][:, :]), deps=[at], dma=True,
                                sem="p2o")
                p.barrier()
                p.flush()
        if upto >= 2:
            s12.close()

        if upto >= 3:
            s35 = ctx.enter_context(ExitStack())
            fT = sb("fT", [128, DC * T], BF16, s35)
            negM = sb("negM", [1, T], BF16, s35)
            with ExitStack() as st:
                W0b = sb("W0b", [128, DC * D], BF16, st)
                xnck = [sb(f"p3xn{i}", [128, DC * 512], BF16, st) for i in range(2)]
                w0src = hap(t_W0, 0, [[D, 128], [128 * D, DC], [1, D]])
                w0t = p.emit("gp", lambda e: e.dma_start(out=W0b[:, :], in_=w0src),
                             dma=True, sem="p3w")
                xn_rel = [None, None]
                for tc in range(TC):
                    i = tc % 2
                    src = t_xnsp.ap()[:, tc * DC * 512:(tc + 1) * DC * 512]
                    p.emit("sp", lambda e, i=i, s=src: e.dma_start(
                        out=xnck[i][:, :], in_=s), deps=[xn_rel[i]], dma=True,
                        sem=f"p3x{i}")
                    ld = p.last(f"p3x{i}")
                    last_ev = None
                    for dout in range(DC):
                        dd = p.bank_take(dout)
                        mmt = None
                        for c in range(DC):
                            mmt = p.emit("pe", lambda e, c=c, dout=dout, i=i: e.matmul(
                                pb[dout][:, :],
                                W0b[:, c * D + dout * 128: c * D + (dout + 1) * 128],
                                xnck[i][:, c * 512:(c + 1) * 512],
                                start=(c == 0), stop=(c == DC - 1)),
                                deps=[ld, w0t] + dd, inc=(c == DC - 1))
                        last_ev = p.bank_read(dout, p.emit(
                            "act", lambda e, dout=dout, tc=tc: e.activation(
                                out=fT[:, dout * T + tc * 512: dout * T + (tc + 1) * 512],
                                in_=pb[dout][:, :], func=AF.Identity,
                                bias=b0v[:, dout:dout + 1]), deps=[mmt]))
                    xn_rel[i] = last_ev
                p.barrier()
                p.flush()

        if upto >= 4:
            f_tok = sb("f_tok", [128, TT * D], BF16, s35)
            band = sb("band", [128, 1152], F32, s35)
            with ExitStack() as st:
                sq = [sb(f"sqf{i}", [128, T], BF16, st) for i in range(2)]
                tk = p.emit("gp", lambda e: e.iota(band[:, :], [[-1, 1152]], base=512,
                                                   channel_multiplier=1,
                                                   allow_small_or_imprecise_dtypes=True))
                tk = p.emit("act", lambda e: e.activation(
                    out=band[:, :], in_=band[:, :], func=AF.Abs), deps=[tk])
                tk = p.emit("dve", lambda e: e.tensor_single_scalar(
                    band[:, :], band[:, :], 10.0, ALU.min), deps=[tk])
                p.emit("dve", lambda e: e.tensor_scalar(
                    band[:, :], band[:, :], 0.01, -0.1, ALU.mult, ALU.add), deps=[tk])
                for c in range(DC):
                    dst = bass.AP(f_tok, c * 128, [[TT * D, 128], [D, TT], [1, 128]])
                    p.emit("sp", lambda e, d=dst, c=c: e.dma_start_transpose(
                        d, fT[:, c * T:(c + 1) * T]), dma=True, sem="p4t")
                mm_last = {}
                prev_sq = None
                for c in range(DC):
                    i = c % 2
                    sqt = p.emit("act", lambda e, c=c, i=i: e.activation(
                        out=sq[i][:, :], in_=fT[:, c * T:(c + 1) * T], func=AF.Square),
                        deps=[prev_sq])
                    for tc in range(TC):
                        dd = p.bank_take(tc) if c == 0 else []
                        tok = p.emit("pe", lambda e, c=c, i=i, tc=tc: e.matmul(
                            pb[tc][0:1, :], ones_col[:, :],
                            sq[i][:, tc * 512:(tc + 1) * 512],
                            start=(c == 0), stop=(c == DC - 1)),
                            deps=[sqt] + dd,
                            inc=(c == DC - 1) or (tc == TC - 1))
                        if c == DC - 1:
                            mm_last[tc] = tok
                    prev_sq = tok
                for tc in range(TC):
                    p.bank_read(tc, p.emit("dve", lambda e, tc=tc: e.tensor_single_scalar(
                        negM[0:1, tc * 512:(tc + 1) * 512], pb[tc][0:1, :], -1.0,
                        ALU.mult), deps=[mm_last[tc]]))
                p.barrier()
                p.flush()

        if upto >= 5:
            with ExitStack() as st:
                at = sb("atil", [128, TT * 512], BF16, st)
                zinv = sb("zinv", [1, 512], F32, st)
                flsc = sb("flsc", [1, 1], F32, st)
                zinvB = sb("zinvB", [128, 512], F32, st)
                negMB = sb("negMB", [128, 512], F32, st)
                pc = [sb(f"pchunk{i}", [128, 512], BF16, st) for i in range(2)]
                sc5 = [sb(f"sc5{i}", [128, 512], F32, st) for i in range(2)]
                x1o = [sb(f"x1o{i}", [128, 512], F32, st) for i in range(2)]
                pc_rel = [None, None]
                x1o_rel = [None, None]
                bandtok = None
                eb = [0, 1, 4, 5]
                nb_prev = None
                for g in range(TC):
                    exp_tok = {}
                    z_tok = None
                    dz = p.bank_take(2)
                    d3b = p.bank_take(3)
                    nb_mm = p.emit("pe", lambda e, g=g: e.matmul(
                        pb[3][:, :], ones_row[0:1, :],
                        negM[0:1, g * 512:(g + 1) * 512],
                        start=True, stop=True), deps=d3b)
                    nb_tok = p.bank_read(3, p.emit(
                        "dve", lambda e: e.tensor_copy(negMB[:, :], pb[3][:, :]),
                        deps=[nb_mm, nb_prev]))

                    def emit_z(tt, g=g, dz=dz):
                        return p.emit("pe", lambda e, tt=tt: e.matmul(
                            pb[2][0:1, :], ones_col[:, :],
                            at[:, tt * 512:(tt + 1) * 512],
                            start=(tt == 0), stop=(tt == TT - 1)),
                            deps=[exp_tok[tt]] + (dz if tt == 0 else []),
                            inc=(tt == TT - 1))

                    for tt in range(TT):
                        bank = eb[tt % 4]
                        dd = p.bank_take(bank)
                        mmt = None
                        for c in range(DC):
                            mmt = p.emit("pe", lambda e, c=c, tt=tt, g=g, bank=bank: e.matmul(
                                pb[bank][:, :],
                                fT[:, c * T + tt * 128: c * T + (tt + 1) * 128],
                                fT[:, c * T + g * 512: c * T + (g + 1) * 512],
                                start=(c == 0), stop=(c == DC - 1)),
                                deps=dd if c == 0 else [], inc=(c == DC - 1))
                        madd = p.emit("dve", lambda e, bank=bank: e.tensor_tensor(
                            pb[bank][:, :], pb[bank][:, :], negMB[:, :],
                            ALU.add), deps=[mmt, nb_tok])
                        nb_prev = madd
                        delta = 128 * tt - 512 * g
                        btok = None
                        if -128 <= delta <= 512:
                            btok = p.emit("dve", lambda e, d=delta, bank=bank: e.tensor_tensor(
                                pb[bank][:, :], pb[bank][:, :], band[:, 512 - d:1024 - d],
                                ALU.add), deps=[madd, bandtok])
                        et = p.bank_read(bank, p.emit(
                            "act", lambda e, tt=tt, bank=bank: e.activation(
                                out=at[:, tt * 512:(tt + 1) * 512], in_=pb[bank][:, :],
                                func=AF.Exp), deps=[btok if btok else madd]))
                        exp_tok[tt] = et
                        if tt >= 1:
                            z_tok = emit_z(tt - 1)
                    flush = p.emit("act", lambda e: e.activation(
                        out=flsc[0:1, 0:1], in_=at[0:1, 0:1], func=AF.Identity),
                        deps=[exp_tok[TT - 1]])
                    z_tok = p.emit("pe", lambda e: e.matmul(
                        pb[2][0:1, :], ones_col[:, :],
                        at[:, (TT - 1) * 512: TT * 512],
                        start=False, stop=True),
                        deps=[exp_tok[TT - 1], flush], inc=True)
                    zt = p.bank_read(2, p.emit(
                        "dve", lambda e: e.reciprocal(zinv[0:1, :], pb[2][0:1, :]),
                        deps=[z_tok]))
                    d3 = p.bank_take(3)
                    zb = p.emit("pe", lambda e: e.matmul(
                        pb[3][:, :], ones_rf[0:1, :], zinv[0:1, :],
                        start=True, stop=True), deps=[zt] + d3)
                    zc = p.bank_read(3, p.emit(
                        "dve", lambda e: e.tensor_copy(zinvB[:, :], pb[3][:, :]),
                        deps=[zb]))
                    for dout in range(DC):
                        bank = 4 + dout % 4
                        dd = p.bank_take(bank)
                        mmt = None
                        for tt in range(TT):
                            mmt = p.emit("pe", lambda e, tt=tt, dout=dout, bank=bank: e.matmul(
                                pb[bank][:, :],
                                f_tok[:, tt * D + dout * 128: tt * D + (dout + 1) * 128],
                                at[:, tt * 512:(tt + 1) * 512],
                                start=(tt == 0), stop=(tt == TT - 1)),
                                deps=[exp_tok[tt]] + (dd if tt == 0 else []),
                                inc=(tt == TT - 1))
                        j = dout % 2
                        psrc = t_pT.ap()[dout * 128:(dout + 1) * 128,
                                         g * 512:(g + 1) * 512]
                        p.emit("gp", lambda e, j=j, s=psrc: e.dma_start(
                            out=pc[j][:, :], in_=s), deps=[pc_rel[j]], dma=True,
                            sem=f"p5p{j}")
                        m1 = p.bank_read(bank, p.emit(
                            "dve", lambda e, j=j, bank=bank: e.tensor_tensor(
                                sc5[j][:, :], pb[bank][:, :], zinvB[:, :], ALU.mult),
                            deps=[mmt, zc]))
                        m2 = p.emit("dve", lambda e, j=j: e.tensor_tensor(
                            x1o[j][:, :], sc5[j][:, :], pc[j][:, :], ALU.add),
                            deps=[m1, p.last(f"p5p{j}"), x1o_rel[j]])
                        pc_rel[j] = m2
                        dst = t_x1T.ap()[:, g * DC * 512 + dout * 512:
                                         g * DC * 512 + (dout + 1) * 512]
                        x1o_rel[j] = p.emit("sp", lambda e, j=j, d=dst: e.dma_start(
                            out=d, in_=x1o[j][:, :]), deps=[m2], dma=True,
                            sem=f"p5o{j}")
                    p.barrier()
                p.flush()
        if upto >= 5 and s35 is not None:
            s35.close()

        if upto >= 6:
            s68 = ctx.enter_context(ExitStack())
            xn1T = sb("xn1T", [128, DC * T], BF16, s68)
            s67 = ctx.enter_context(ExitStack())
            binb = sb("binb", [1, WID], BF16, s67)
            p.emit("gp", lambda e: e.dma_start(out=binb[:, :], in_=t_binr.ap()),
                   dma=True, sem="p6b")
            ln_pass(t_x1T, sink_sbuf=xn1T)

        if upto >= 7:
            with ExitStack() as st:
                WinTb = sb("WinTb", [128, DC * WID], BF16, st)
                WoutTb = sb("WoutTb", [128, WK * D], BF16, st)
                silu_t = [sb(f"silu_t{i}", [128, WID], BF16, st) for i in range(2)]
                dec_t = [sb(f"dec_t{i}", [128, WID], BF16, st) for i in range(2)]
                mask_t = [sb(f"mask_t{i}", [128, WID], BF16, st) for i in range(2)]
                masked_t = [sb(f"masked_t{i}", [128, WID], BF16, st) for i in range(2)]
                mTc = [sb(f"mTc{i}", [128, WK * 512], BF16, st) for i in range(2)]
                xrs = [sb(f"p7xr{i}", [128, 512], F32, st) for i in range(2)]
                p1o = [sb(f"p1o{i}", [128, 512], BF16, st) for i in range(2)]
                wis = hap(t_WinT, 0, [[WID, 128], [128 * WID, DC], [1, WID]])
                p.emit("gp", lambda e: e.dma_start(out=WinTb[:, :], in_=wis),
                       dma=True, sem="p7w")
                wos = hap(t_WoutT, 0, [[D, 128], [128 * D, WK], [1, D]])
                p.emit("gp", lambda e: e.dma_start(out=WoutTb[:, :], in_=wos),
                       dma=True, sem="p7w")
                wt1 = wt2 = p.last("p7w")
                xr_rel = [wt1, wt1]
                po_rel = [None, None]
                mtc_rel = [None, None]
                tr_all = [None] * TC
                route_rel = [None, None]

                prev_tr = [None, None]

                def stage_a_tt(tcc, ii):
                    i = tcc % 2
                    if True:
                        tt = tcc * 4 + ii
                        i2 = tt % 2
                        sd_toks = []
                        for wc in range(WC):
                            bank = wc % 4
                            dd = p.bank_take(bank)
                            for c in range(DC):
                                p.emit("pe", lambda e, c=c, tt=tt, wc=wc, bank=bank: e.matmul(
                                    pb[bank][:, :],
                                    xn1T[:, c * T + tt * 128: c * T + (tt + 1) * 128],
                                    WinTb[:, c * WID + wc * 512: c * WID + (wc + 1) * 512],
                                    start=(c == 0), stop=False),
                                    deps=[wt1] + (dd if c == 0 else []), inc=False)
                            mmt = p.emit("pe", lambda e, wc=wc, bank=bank: e.matmul(
                                pb[bank][:, :], ones_row[0:1, :],
                                binb[0:1, wc * 512:(wc + 1) * 512],
                                start=False, stop=True), inc=True)
                            s1 = p.bank_read(bank, p.emit(
                                "act", lambda e, wc=wc, i2=i2, bank=bank: e.activation(
                                    out=silu_t[i2][:, wc * 512:(wc + 1) * 512],
                                    in_=pb[bank][:, :], func=AF.Silu),
                                deps=[mmt, route_rel[i2]]))
                            s2 = p.bank_read(bank, p.emit(
                                "dve", lambda e, wc=wc, i2=i2, bank=bank: e.tensor_single_scalar(
                                    dec_t[i2][:, wc * 512:(wc + 1) * 512], pb[bank][:, :],
                                    0.0, ALU.is_gt), deps=[mmt, route_rel[i2], s1]))
                            sd_toks += [s1, s2]
                        mt = p.emit("dve", lambda e, i2=i2: e.memset(
                            bass.AP(mask_t[i2], NNODE, [[WID, 128], [256, NTREE], [1, 1]]),
                            0.0), deps=sd_toks + [prev_tr[i2]])
                        mt = p.emit("dve", lambda e, i2=i2: e.memset(
                            bass.AP(mask_t[i2], 0, [[WID, 128], [256, NTREE], [1, 1]]),
                            1.0), deps=[mt])
                        for lev in range(7):
                            o = (1 << lev) - 1
                            o2 = (1 << (lev + 1)) - 1
                            cnt = 1 << lev
                            pv = bass.AP(mask_t[i2], o, [[WID, 128], [256, NTREE], [1, cnt]])
                            dv = bass.AP(dec_t[i2], o, [[WID, 128], [256, NTREE], [1, cnt]])
                            ov = bass.AP(mask_t[i2], o2 + cnt,
                                         [[WID, 128], [256, NTREE], [1, cnt]])
                            ev_ = bass.AP(mask_t[i2], o2,
                                          [[WID, 128], [256, NTREE], [1, cnt]])
                            mt = p.emit("dve", lambda e, pv=pv, dv=dv, ov=ov: e.tensor_tensor(
                                ov, pv, dv, ALU.mult), deps=[mt])
                            mt = p.emit("dve", lambda e, pv=pv, ov=ov, ev_=ev_: e.tensor_tensor(
                                ev_, pv, ov, ALU.subtract), deps=[mt])
                        mk = p.emit("dve", lambda e, i2=i2: e.tensor_tensor(
                            masked_t[i2][:, :], silu_t[i2][:, :], mask_t[i2][:, :],
                            ALU.mult), deps=[mt])
                        route_rel[i2] = mk
                        dstm = bass.AP(mTc[i], ii * 128,
                                       [[WK * 512, 128], [512, WK], [1, 128]])
                        trk = p.emit("sp", lambda e, d=dstm, i=i, i2=i2: e.dma_start_transpose(
                            d, masked_t[i2][:, :]),
                            deps=[mk, mtc_rel[i]] if ii == 0 else [mk], dma=True,
                            sem=f"p7t{i2}")
                        prev_tr[i2] = trk
                        if ii == 3:
                            tr_all[tcc] = [prev_tr[0], prev_tr[1]]

                def stage_b_d2(tcc, k):
                    i = tcc % 2
                    for dout in (2 * k, 2 * k + 1):
                        bank = 4 + dout % 4
                        dd = p.bank_take(bank)
                        mmt = None
                        for wc in range(WK):
                            mmt = p.emit("pe", lambda e, wc=wc, dout=dout, i=i, bank=bank: e.matmul(
                                pb[bank][:, :],
                                WoutTb[:, wc * D + dout * 128: wc * D + (dout + 1) * 128],
                                mTc[i][:, wc * 512:(wc + 1) * 512],
                                start=(wc == 0), stop=(wc == WK - 1)),
                                deps=[wt2] + tr_all[tcc] + (dd if wc == 0 else []),
                                inc=(wc == WK - 1))
                        j = dout % 2
                        xsrc = t_x1T.ap()[:, tcc * DC * 512 + dout * 512:
                                          tcc * DC * 512 + (dout + 1) * 512]
                        p.emit("gp", lambda e, j=j, s=xsrc: e.dma_start(
                            out=xrs[j][:, :], in_=s), deps=[xr_rel[j]], dma=True,
                            sem=f"p7r{j}")
                        at2 = p.bank_read(bank, p.emit(
                            "dve", lambda e, j=j, bank=bank: e.tensor_tensor(
                                p1o[j][:, :], pb[bank][:, :], xrs[j][:, :], ALU.add),
                            deps=[mmt, p.last(f"p7r{j}"), po_rel[j]]))
                        xr_rel[j] = at2
                        dst = t_p1T.ap()[dout * 128:(dout + 1) * 128,
                                         tcc * 512:(tcc + 1) * 512]
                        po_rel[j] = p.emit("sp", lambda e, j=j, d=dst: e.dma_start(
                            out=d, in_=p1o[j][:, :]), deps=[at2], dma=True,
                            sem=f"p7o{j}")
                        if dout == DC - 1:
                            mtc_rel[i] = mmt

                for tcc in range(TC):
                    for ii in range(4):
                        stage_a_tt(tcc, ii)
                        if tcc >= 1:
                            stage_b_d2(tcc - 1, ii)
                for k in range(4):
                    stage_b_d2(TC - 1, k)
                p.barrier()
                p.flush()
        if upto >= 7:
            s67.close()

        if upto >= 8:
            with ExitStack() as st:
                WvTb = sb("WvTb", [128, DC * D], BF16, st)
                WoTb = sb("WoTb", [128, DC * D], BF16, st)
                vsub = sb("vsub", [128, 1024], F32, st)
                fsc = sb("fsc", [128, DC], F32, st)
                zofft = sb("zofft", [128, 4 * DC], F32, st)
                cstk = sb("cstk", [128, 4], F32, st)
                carr = sb("carr", [128, 2 * DC], F32, st)
                tr = sb("tramp", [128, 1024], F32, st)
                s_q = [sb(f"s_q{i}", [128, 1024], F32, st) for i in range(2)]
                s_n = [sb(f"s_n{i}", [128, 1024], F32, st) for i in range(2)]
                s_f = [sb(f"s_f{i}", [128, 1024], F32, st) for i in range(2)]
                s_sin = [sb(f"s_sin{i}", [128, 1024], F32, st) for i in range(2)]
                s_cos = [sb(f"s_cos{i}", [128, 1024], F32, st) for i in range(2)]
                cA = [sb(f"cA{i}", [128, 1024], F32, st) for i in range(2)]
                cB = [sb(f"cB{i}", [128, 1024], F32, st) for i in range(2)]
                rp_a = sb("rp_a", [128, 1024], BF16, st)
                rp_b = sb("rp_b", [128, 1024], BF16, st)
                ret8 = [[sb(f"ret8_{pp}_{j}", [128, 1024], BF16, st)
                         for j in range(DC)] for pp in range(2)]
                p1c = [sb(f"p1c8_{i}", [128, 512], BF16, st) for i in range(4)]
                osb = [sb("osb8_0", [128, 512], F32, st)] * 2
                onesT = sb("onesT", [128, 1024], BF16, st)
                ot = p.emit("dve", lambda e: e.memset(onesT[:, :], 1.0))
                p.emit("dve", lambda e: e.memset(cstk[:, 0:1], MAGIC))
                p.emit("dve", lambda e: e.memset(cstk[:, 1:2], -MAGIC))
                ck = p.emit("dve", lambda e: e.memset(cstk[:, 2:3], 0.25))
                fk = p.emit("dve", lambda e: e.tensor_single_scalar(
                    fsc[:, :], freqv[:, :], INV2PI, ALU.mult))
                zkt = []
                for z in range(4):
                    zkt.append(p.emit("dve", lambda e, z=z: e.tensor_single_scalar(
                        zofft[:, z * DC:(z + 1) * DC], fsc[:, :],
                        1024.0 * z, ALU.mult), deps=[fk]))
                wvl = hap(t_WvT, 0, [[D, 128], [128 * D, DC], [1, D]])
                p.emit("gp", lambda e: e.dma_start(out=WvTb[:, :], in_=wvl),
                       dma=True, sem="p8w")
                wol = hap(t_WoT, 0, [[D, 128], [128 * D, DC], [1, D]])
                p.emit("gp", lambda e: e.dma_start(out=WoTb[:, :], in_=wol),
                       dma=True, sem="p8w")
                trt = p.emit("gp", lambda e: e.iota(
                    tr[:, :], [[1, 1024]], base=0, channel_multiplier=0,
                    allow_small_or_imprecise_dtypes=True))
                wvt = wot = p.last("p8w")
                prev = {}
                ret_tok = [[None] * DC, [None] * DC]
                ret_rel = [None, None]
                pend_ret = [None]
                p1_rel = [None] * 4
                os_rel = [None, None]
                pk = [0, 0]
                kk_ = [0]

                st8 = {}

                def chainA(ts_, j):
                    g = kk_[0] % 2
                    kk_[0] += 1
                    mm_v = []
                    for h2 in range(2):
                        tc = ts_ * 2 + h2
                        bank = kk_[0] % 2 * 2 + h2
                        dd = p.bank_take(bank)
                        mmt = None
                        for c in range(DC):
                            mmt = p.emit("pe", lambda e, c=c, j=j, tc=tc, bank=bank: e.matmul(
                                pb[bank][:, :],
                                WvTb[:, c * D + j * 128: c * D + (j + 1) * 128],
                                xn1T[:, c * T + tc * 512: c * T + (tc + 1) * 512],
                                start=(c == 0), stop=(c == DC - 1)),
                                deps=[wvt] + (dd if c == 0 else []),
                                inc=(c == DC - 1))
                        mm_v.append((bank, mmt))
                    t1 = p.emit("act", lambda e, g=g, j=j, z=ts_: e.activation(
                        out=s_q[g][:, :], in_=tr[:, :], func=AF.Identity,
                        scale=fsc[:, j:j + 1],
                        bias=zofft[:, z * DC + j: z * DC + j + 1]),
                        deps=[trt, fk, zkt[ts_], prev.get(("q", g))])
                    t2a = p.emit("act", lambda e, g=g: e.activation(
                        out=s_n[g][:, :], in_=s_q[g][:, :], func=AF.Identity,
                        bias=cstk[:, 0:1]), deps=[t1, ck, prev.get(("n", g))])
                    t2b = p.emit("act", lambda e, g=g: e.activation(
                        out=s_n[g][:, :], in_=s_n[g][:, :], func=AF.Identity,
                        bias=cstk[:, 1:2]), deps=[t2a])
                    t3 = p.emit("dve", lambda e, g=g: e.tensor_tensor(
                        s_f[g][:, :], s_q[g][:, :], s_n[g][:, :], ALU.subtract),
                        deps=[t2b, prev.get(("f", g))])
                    st8[(ts_, j)] = (g, mm_v, t3)

                def chainB(ts_, j):
                    g, mm_v, t3 = st8.pop((ts_, j))
                    rp = ts_ % 2
                    if pend_ret[0] is not None:
                        fn = pend_ret[0]
                        pend_ret[0] = None
                        fn()
                    evs = []
                    for h2, (bank, mmt) in enumerate(mm_v):
                        evs.append(p.bank_read(bank, p.emit(
                            "act", lambda e, h2=h2, bank=bank: e.activation(
                                out=vsub[:, h2 * 512:(h2 + 1) * 512],
                                in_=pb[bank][:, :], func=AF.Identity),
                            deps=[mmt, prev.get("vsub")])))
                    sn = p.emit("act", lambda e, g=g: e.activation(
                        out=s_sin[g][:, :], in_=s_f[g][:, :], func=AF.Sin,
                        scale=TWOPI), deps=[t3, prev.get(("sin", g))])
                    t4 = p.emit("act", lambda e, g=g: e.activation(
                        out=s_q[g][:, :], in_=s_f[g][:, :], func=AF.Identity,
                        bias=cstk[:, 2:3]), deps=[t3])
                    t5a = p.emit("act", lambda e, g=g: e.activation(
                        out=s_n[g][:, :], in_=s_q[g][:, :], func=AF.Identity,
                        bias=cstk[:, 0:1]), deps=[t4, t3])
                    t5b = p.emit("act", lambda e, g=g: e.activation(
                        out=s_n[g][:, :], in_=s_n[g][:, :], func=AF.Identity,
                        bias=cstk[:, 1:2]), deps=[t5a])
                    t6 = p.emit("dve", lambda e, g=g: e.tensor_tensor(
                        s_f[g][:, :], s_q[g][:, :], s_n[g][:, :], ALU.subtract),
                        deps=[t5b, sn])
                    cs = p.emit("act", lambda e, g=g: e.activation(
                        out=s_cos[g][:, :], in_=s_f[g][:, :], func=AF.Sin,
                        scale=TWOPI), deps=[t6, prev.get(("cos", g))])
                    vr1 = p.emit("dve", lambda e, g=g: e.tensor_tensor(
                        s_q[g][:, :], vsub[:, :], s_cos[g][:, :], ALU.mult),
                        deps=[cs, t6] + evs)
                    vr2 = p.emit("dve", lambda e, g=g: e.tensor_tensor(
                        s_n[g][:, :], vsub[:, :], s_sin[g][:, :], ALU.mult),
                        deps=[sn, t6, vr1])
                    prev["vsub"] = vr2
                    ia = 0.0 if ts_ == 0 else carr[:, 2 * j: 2 * j + 1]
                    ib = 0.0 if ts_ == 0 else carr[:, 2 * j + 1: 2 * j + 2]
                    sc_a = p.emit("dve", lambda e, k=j % 2, g=g, ia=ia: e.tensor_tensor_scan(
                        cA[k][:, :], onesT[:, :], s_q[g][:, :], ia, ALU.mult,
                        ALU.add),
                        deps=[vr1, ot, prev.get(("car", j)), prev.get(("cA", j % 2))])
                    sc_b = p.emit("dve", lambda e, k=j % 2, g=g, ib=ib: e.tensor_tensor_scan(
                        cB[k][:, :], onesT[:, :], s_n[g][:, :], ib, ALU.mult,
                        ALU.add), deps=[vr2, prev.get(("cB", j % 2))])
                    ca = p.emit("dve", lambda e, k=j % 2, j=j: e.tensor_copy(
                        carr[:, 2 * j: 2 * j + 1], cA[k][:, 1023:1024]),
                        deps=[sc_a])
                    cb = p.emit("dve", lambda e, k=j % 2, j=j: e.tensor_copy(
                        carr[:, 2 * j + 1: 2 * j + 2], cB[k][:, 1023:1024]),
                        deps=[sc_b])
                    prev[("car", j)] = cb
                    r1 = p.emit("gp", lambda e, k=j % 2, g=g: e.tensor_tensor(
                        rp_a[:, :], s_cos[g][:, :], cA[k][:, :], ALU.mult),
                        deps=[sc_a, ca, prev.get("rp")])
                    r2 = p.emit("gp", lambda e, k=j % 2, g=g: e.tensor_tensor(
                        rp_b[:, :], s_sin[g][:, :], cB[k][:, :], ALU.mult),
                        deps=[sc_b, cb])

                    def emit_ret(rp=rp, j=j, r1=r1, r2=r2):
                        ret = p.emit("dve", lambda e, rp=rp, j=j: e.tensor_tensor(
                            ret8[rp][j][:, :], rp_a[:, :], rp_b[:, :], ALU.add),
                            deps=[r1, r2, ret_rel[rp]])
                        ret_tok[rp][j] = ret
                        prev["rp"] = ret
                    pend_ret[0] = emit_ret
                    prev[("q", g)] = sc_a
                    prev[("n", g)] = sc_b
                    prev[("f", g)] = cs
                    prev[("sin", g)] = r2
                    prev[("cos", g)] = r1
                    prev[("cA", j % 2)] = r1
                    prev[("cB", j % 2)] = r2

                def wave(ts_, wv):
                    rp = ts_ % 2
                    half, grp = wv // 2, wv % 2
                    co = ts_ * 1024 + half * 512
                    lts = []
                    for dd_ in range(4):
                        dout = grp * 4 + dd_
                        i4 = pk[0] % 4
                        pk[0] += 1
                        psrc = t_p1T.ap()[dout * 128:(dout + 1) * 128, co:co + 512]
                        lts.append((i4, p.emit("sp", lambda e, i4=i4, s=psrc: e.dma_start(
                            out=p1c[i4][:, :], in_=s), deps=[p1_rel[i4]],
                            dma=True, sem=f"p8p{i4}")))
                    mms = []
                    for dd_ in range(4):
                        dout = grp * 4 + dd_
                        bank = 4 + dd_
                        dbt = p.bank_take(bank)
                        mmt = None
                        for j in range(DC):
                            mmt = p.emit("pe", lambda e, j=j, dout=dout, half=half, bank=bank, rp=rp: e.matmul(
                                pb[bank][:, :],
                                WoTb[:, j * D + dout * 128: j * D + (dout + 1) * 128],
                                ret8[rp][j][:, half * 512:(half + 1) * 512],
                                start=(j == 0), stop=(j == DC - 1)),
                                deps=[wot, ret_tok[rp][j]] + (dbt if j == 0 else []),
                                inc=(j == DC - 1))
                        mms.append((bank, mmt))
                    ld_all = [p.last(f"p8p{k}") for k in range(4)]
                    for dd_ in range(4):
                        dout = grp * 4 + dd_
                        bank, mmt = mms[dd_]
                        i4, lt = lts[dd_]
                        oi = pk[1] % 2
                        pk[1] += 1
                        at3 = p.bank_read(bank, p.emit(
                            "dve", lambda e, i4=i4, oi=oi, bank=bank: e.tensor_tensor(
                                osb[oi][:, :], pb[bank][:, :], p1c[i4][:, :],
                                ALU.add), deps=[mmt, os_rel[0], os_rel[1]]
                            + ld_all))
                        p1_rel[i4] = at3
                        dst = t_out.ap()[dout * 128:(dout + 1) * 128, co:co + 512]
                        os_rel[oi] = p.emit("sp", lambda e, oi=oi, d=dst: e.dma_start(
                            out=d, in_=osb[oi][:, :]), deps=[at3], dma=True,
                            sem="p8o")
                    if wv == 3:
                        ret_rel[rp] = mms[-1][1]

                seq = [(ts_, j) for ts_ in range(4) for j in range(DC)]
                for idx, (ts_, j) in enumerate(seq):
                    chainA(ts_, j)
                    if idx >= 1:
                        pt, pj = seq[idx - 1]
                        chainB(pt, pj)
                        if pt >= 1 and pj % 2 == 1:
                            wave(pt - 1, pj // 2)
                chainB(3, DC - 1)
                if pend_ret[0] is not None:
                    fn = pend_ret[0]
                    pend_ret[0] = None
                    fn()
                wave(2, 3)
                for wv in range(4):
                    wave(3, wv)
                p.barrier()
                p.flush()
        if upto >= 8 and s68 is not None:
            s68.close()

        p.flush()

    return nc


def prep_inputs(inputs):
    f = np.float32
    x = np.asarray(inputs["L0_attn_x"], f)
    W0 = np.ascontiguousarray(np.asarray(inputs["L0_attn_W"], f))
    b0 = np.asarray(inputs["L0_attn_b"], f)
    w1 = np.asarray(inputs["L0_ffn_w1"], f)
    b1 = np.asarray(inputs["L0_ffn_b1"], f)
    w2 = np.asarray(inputs["L0_ffn_w2"], f)
    b2 = np.asarray(inputs["L0_ffn_b2"], f)
    Wv = np.asarray(inputs["L1_attn_Wv"], f)
    Wo = np.asarray(inputs["L1_attn_Wo"], f)
    freqs = np.asarray(inputs["L1_attn_freqs"], f)
    Win = np.asarray(inputs["L1_ffn_W_in"], f)
    bin_ = np.asarray(inputs["L1_ffn_b_in"], f)
    Wout = np.asarray(inputs["L1_ffn_W_out"], f)

    shared = {
        "W0": W0,
        "b0v": np.ascontiguousarray(b0.reshape(DC, 128).T),
        "w1n": np.ascontiguousarray(w1),
        "w1t": np.ascontiguousarray(w1.T),
        "b1v": np.ascontiguousarray(b1.reshape(HC, 128).T),
        "w2n": np.ascontiguousarray(w2),
        "w2t": np.ascontiguousarray(w2.T),
        "b2v": np.ascontiguousarray(b2.reshape(DC, 128).T),
        "WvT": np.ascontiguousarray(Wv.T),
        "WoT": np.ascontiguousarray(Wo.T),
        "freqv": np.ascontiguousarray(freqs.reshape(DC, 128).T),
    }
    WinP = np.zeros((WID, D), f)
    binP = np.zeros((1, WID), f)
    WoutP = np.zeros((D, WID), f)
    wr = Win.reshape(NTREE, NNODE, D)
    br = bin_.reshape(NTREE, NNODE)
    wo = Wout.reshape(D, NTREE, NNODE)
    node_at = np.zeros(NNODE, np.int64)
    curj = [0]
    for lev in range(7):
        nxt = [2 * j for j in curj] + [2 * j + 1 for j in curj]
        o2 = (1 << (lev + 1)) - 1
        for k, j in enumerate(nxt):
            node_at[o2 + k] = o2 + j
        curj = nxt
    for t in range(NTREE):
        WinP[t * 256: t * 256 + NNODE, :] = wr[t][node_at]
        binP[0, t * 256: t * 256 + NNODE] = br[t][node_at]
        WoutP[:, t * 256: t * 256 + NNODE] = wo[:, t, :][:, node_at]
    shared["WinT"] = np.ascontiguousarray(WinP.T)
    shared["binr"] = binP
    shared["WoutT"] = np.ascontiguousarray(WoutP.T)

    in_maps = []
    for b in range(8):
        m = dict(shared)
        m["xT"] = np.ascontiguousarray(
            x[b].reshape(TC, 512, DC, 128).transpose(3, 0, 2, 1).reshape(128, TC * DC * 512))
        in_maps.append(m)
    return in_maps


def kernel(**inputs):
    in_maps = prep_inputs(inputs)
    nc = build_program(debug=False, upto=99)
    res = run_bass_kernel_spmd(nc, in_maps, core_ids=list(range(8)))
    out = np.stack([np.asarray(r["out"], np.float32).T for r in res.results], axis=0)
    return out


# version 39
# speedup vs baseline: 1.0949x; 1.0949x over previous
import numpy as np
from contextlib import ExitStack

import concourse.bass as bass
import concourse.mybir as mybir
from concourse.bass_utils import run_bass_kernel_spmd

F32 = mybir.dt.float32
BF16 = mybir.dt.bfloat16
AF = mybir.ActivationFunctionType
ALU = mybir.AluOpType

D = 1024
T = 4096
H = 4096
WID = 2048
NTREE = 8
NNODE = 255
DC = D // 128
TC = T // 512
TT = T // 128
HC = H // 128
WC = WID // 512
WK = WID // 128

MAGIC = 12582912.0
INV2PI = float(np.float32(1.0 / (2.0 * np.pi)))
CW1 = 6.28125
CW2 = float(np.float32(2.0 * np.pi - 6.28125))
PI = float(np.float32(np.pi))
HALFPI = float(np.float32(np.pi / 2.0))
TWOPI = float(np.float32(2.0 * np.pi))
LN_EPS = 1e-5


class Prog:

    ENGS = ("pe", "act", "dve", "sp", "gp")

    def __init__(self, nc, ctx):
        self.nc = nc
        self.ctx = ctx
        self.sem = {e: ctx.enter_context(nc.semaphore("sem_" + e)) for e in self.ENGS}
        self.cnt = {e: 0 for e in self.ENGS}
        self.wm = {e: {} for e in self.ENGS}
        self.q = {e: [] for e in self.ENGS}
        self.bankrel = {}

    def emit(self, eng, fn, deps=(), inc=True, dma=False, sem=None):
        key = sem if sem is not None else eng
        if key not in self.sem:
            self.sem[key] = self.ctx.enter_context(
                self.nc.semaphore("sem_" + key))
            self.cnt[key] = 0
        waits = []
        for d in deps:
            if not d:
                continue
            s, v = d
            if v > self.wm[eng].get(s, 0):
                waits.append((s, v))
                self.wm[eng][s] = v
        step = 16 if dma else 1
        tok = None
        if inc:
            self.cnt[key] += step
            tok = (key, self.cnt[key])
        semmap = self.sem
        myname = key

        def closure(engine_obj, waits=tuple(waits), fn=fn, tok=tok, step=step):
            for (s, v) in waits:
                engine_obj.wait_ge(semmap[s], v)
            inst = fn(engine_obj)
            if tok is not None:
                inst.then_inc(semmap[myname], step)

        self.q[eng].append(closure)
        return tok

    def last(self, eng):
        return (eng, self.cnt[eng]) if self.cnt.get(eng, 0) > 0 else None

    def bank_take(self, bid):
        return self.bankrel.pop(bid, [])

    def bank_read(self, bid, tok):
        self.bankrel.setdefault(bid, []).append(tok)
        return tok

    def barrier(self):
        toks = [(k, self.cnt[k]) for k in self.cnt if self.cnt[k] > 0]
        semmap = self.sem
        for e in self.ENGS:
            def closure(engine_obj, toks=tuple(toks), e=e):
                for (s, v) in toks:
                    engine_obj.wait_ge(semmap[s], v)
            self.q[e].append(closure)
            for (s, v) in toks:
                if v > self.wm[e].get(s, 0):
                    self.wm[e][s] = v
        self.bankrel = {}
        return toks

    def flush(self):
        nc = self.nc
        qs = {e: self.q[e] for e in self.ENGS}
        self.q = {e: [] for e in self.ENGS}
        if not any(qs.values()):
            return
        with nc.Block() as block:
            @block.tensor
            def _(e, fns=qs["pe"]):
                for f in fns:
                    f(e)

            @block.scalar
            def _(e, fns=qs["act"]):
                for f in fns:
                    f(e)

            @block.vector
            def _(e, fns=qs["dve"]):
                for f in fns:
                    f(e)

            @block.sync
            def _(e, fns=qs["sp"]):
                for f in fns:
                    f(e)

            @block.gpsimd
            def _(e, fns=qs["gp"]):
                for f in fns:
                    f(e)


def hap(t, off, dims):
    return bass.AP(t, off, [list(d) for d in dims])


P7CUT = 99


def build_program(debug=False, upto=99):
    nc = bass.Bass("TRN2", target_bir_lowering=False)

    t_xT = nc.dram_tensor("xT", [128, TC * DC * 512], F32, kind="ExternalInput")
    t_W0 = nc.dram_tensor("W0", [D, D], F32, kind="ExternalInput")
    t_b0v = nc.dram_tensor("b0v", [128, DC], F32, kind="ExternalInput")
    t_w1n = nc.dram_tensor("w1n", [H, D], F32, kind="ExternalInput")
    t_w1t = nc.dram_tensor("w1t", [D, H], F32, kind="ExternalInput")
    t_b1v = nc.dram_tensor("b1v", [128, HC], F32, kind="ExternalInput")
    t_w2n = nc.dram_tensor("w2n", [D, H], F32, kind="ExternalInput")
    t_w2t = nc.dram_tensor("w2t", [H, D], F32, kind="ExternalInput")
    t_b2v = nc.dram_tensor("b2v", [128, DC], F32, kind="ExternalInput")
    t_WvT = nc.dram_tensor("WvT", [D, D], F32, kind="ExternalInput")
    t_WoT = nc.dram_tensor("WoT", [D, D], F32, kind="ExternalInput")
    t_freqv = nc.dram_tensor("freqv", [128, DC], F32, kind="ExternalInput")
    t_WinT = nc.dram_tensor("WinT", [D, WID], F32, kind="ExternalInput")
    t_binr = nc.dram_tensor("binr", [1, WID], F32, kind="ExternalInput")
    t_WoutT = nc.dram_tensor("WoutT", [WID, D], F32, kind="ExternalInput")
    t_out = nc.dram_tensor("out", [D, T], F32, kind="ExternalOutput")

    dbg = "ExternalOutput" if debug else "Internal"
    t_xnsp = nc.dram_tensor("xnsp", [128, TC * DC * 512], BF16, kind=dbg)
    t_pT = nc.dram_tensor("pT", [D, T], BF16, kind=dbg)
    t_x1T = nc.dram_tensor("x1T", [128, TC * DC * 512], F32, kind=dbg)
    t_p1T = nc.dram_tensor("p1T", [D, T], BF16, kind=dbg)

    with ExitStack() as ctx:
        p = Prog(nc, ctx)

        _sbn = [0]

        def sb(name, shape, dt, st=None, side=None):
            _sbn[0] += 1
            kw = {"side": side} if side else {}
            return (st or ctx).enter_context(
                nc.sbuf_tensor(f"sb{_sbn[0]}_{name}", shape, dt, **kw))

        pb = [ctx.enter_context(nc.psum_tensor(f"pb{i}", [128, 512], F32))
              for i in range(8)]

        ones_col = sb("ones_col", [128, 1], BF16)
        ones_row = sb("ones_row", [1, 128], BF16)
        ones_rf = sb("ones_rf", [1, 128], F32)
        b0v = sb("b0v", [128, DC], F32)
        b1v = sb("b1v", [128, HC], F32)
        b2v = sb("b2v", [128, DC], F32)
        freqv = sb("freqv", [128, DC], F32)
        alpha1 = sb("alpha1", [128, HC], F32)
        alpha2 = sb("alpha2", [128, DC], F32)

        p.emit("dve", lambda e: e.memset(ones_col[:, :], 1.0))
        p.emit("dve", lambda e: e.memset(ones_row[:, :], 1.0))
        p.emit("dve", lambda e: e.memset(ones_rf[:, :], 1.0))
        p.emit("sp", lambda e: e.dma_start(out=b0v[:, :], in_=t_b0v.ap()), dma=True)
        p.emit("sp", lambda e: e.dma_start(out=b1v[:, :], in_=t_b1v.ap()), dma=True)
        p.emit("sp", lambda e: e.dma_start(out=b2v[:, :], in_=t_b2v.ap()), dma=True)
        p.emit("sp", lambda e: e.dma_start(out=freqv[:, :], in_=t_freqv.ap()), dma=True)
        p.barrier()

        def ln_pass(src_dram, sink_dram=None, sink_sbuf=None, units=None):
            units = list(units or [])
            nsteps = TC + 2
            nun = max(1, (len(units) + nsteps - 1) // nsteps) if units else 0
            with ExitStack() as st:
                xc = [sb(f"ln_xc{i}", [128, DC * 512], F32, st) for i in range(4)]
                xcb = [sb(f"ln_xcb{i}", [128, T], BF16, st) for i in range(2)]
                xsq = [sb("ln_xsq0", [128, T], BF16, st)] * 2
                scr = sb("ln_scr", [128, 512], F32, st)
                mrow = [sb(f"ln_mrow{i}", [1, 512], F32, st) for i in range(2)]
                rrow = [sb(f"ln_rrow{i}", [1, 512], F32, st) for i in range(2)]
                s3 = [sb(f"ln_s3{i}", [1, 512], F32, st) for i in range(2)]
                s4 = [sb(f"ln_s4{i}", [1, 512], F32, st) for i in range(2)]
                xnc = [sb("ln_xnc0", [128, DC * 512], BF16, st)] * 2
                ld_tok = [None] * 4
                xc_rel = [None] * 4
                scr_rel = [None]
                xnc_rel = [None]
                t12_tok = {}
                mm_last = {}
                t5_tok = {}
                t7_tok = {}
                t89_tok = {}
                SBANKS = [(4, 5), (2, 3)]
                BBANKS = [(6, 7), (0, 1)]
                ldeng = "gp" if src_dram.dtype == BF16 else "sp"

                def emit_ld(tc):
                    i3 = tc % 4
                    src = src_dram.ap()[:, tc * DC * 512:(tc + 1) * DC * 512]
                    p.emit(
                        ldeng, lambda e, o=xc[i3], s=src: e.dma_start(out=o[:, :], in_=s),
                        deps=[xc_rel[i3]], dma=True, sem=f"lnx{i3}")
                    ld_tok[i3] = p.last(f"lnx{i3}")

                def stage_a(tc):
                    i = tc % 2
                    i3 = tc % 4
                    sb0, sb1 = SBANKS[i]
                    ct = p.emit("act", lambda e, i=i, i3=i3: e.activation(
                        out=xcb[i][:, :], in_=xc[i3][:, :], func=AF.Identity),
                        deps=[ld_tok[i3], mm_last.get(tc - 2)])
                    sq = p.emit("dve", lambda e, i=i: e.tensor_tensor(
                        xsq[i][:, :], xcb[i][:, :], xcb[i][:, :], ALU.mult),
                        deps=[ct, mm_last.get(tc - 1)])
                    d4 = p.bank_take(sb0)
                    d5 = p.bank_take(sb1)
                    for c in range(DC):
                        p.emit("pe", lambda e, c=c, i=i, sb0=sb0: e.matmul(
                            pb[sb0][0:1, :], ones_col[:, :],
                            xcb[i][:, c * 512:(c + 1) * 512],
                            start=(c == 0), stop=(c == DC - 1)),
                            deps=[ct] + d4, inc=False)
                    mm_tok = None
                    for c in range(DC):
                        mm_tok = p.emit("pe", lambda e, c=c, i=i, sb1=sb1: e.matmul(
                            pb[sb1][0:1, :], ones_col[:, :],
                            xsq[i][:, c * 512:(c + 1) * 512],
                            start=(c == 0), stop=(c == DC - 1)),
                            deps=[sq] + d5, inc=(c == DC - 1))
                    mm_last[tc] = mm_tok
                    t89p = t89_tok.get(tc - 2, (None, None))
                    t1 = p.bank_read(sb0, p.emit("dve", lambda e, i=i, sb0=sb0: e.tensor_single_scalar(
                        mrow[i][0:1, :], pb[sb0][0:1, :], 1.0 / D, ALU.mult),
                        deps=[mm_tok, t7_tok.get(tc - 2)]))
                    t2 = p.bank_read(sb1, p.emit("dve", lambda e, i=i, sb1=sb1: e.tensor_single_scalar(
                        s3[i][0:1, :], pb[sb1][0:1, :], 1.0 / D, ALU.mult),
                        deps=[mm_tok, t89p[1]]))
                    t12_tok[tc] = (t1, t2)

                def stage_b(tc):
                    i = tc % 2
                    bb0, bb1 = BBANKS[i]
                    t1, t2 = t12_tok.pop(tc)
                    t89p = t89_tok.get(tc - 2, (None, None))
                    t3 = p.emit("dve", lambda e, i=i: e.tensor_tensor(
                        s4[i][0:1, :], mrow[i][0:1, :], mrow[i][0:1, :], ALU.mult),
                        deps=[t1, t5_tok.get(tc - 2)])
                    t4 = p.emit("dve", lambda e, i=i: e.tensor_tensor(
                        s3[i][0:1, :], s3[i][0:1, :], s4[i][0:1, :], ALU.subtract),
                        deps=[t2, t3])
                    t4b = p.emit("dve", lambda e, i=i: e.tensor_single_scalar(
                        s4[i][0:1, :], s3[i][0:1, :], LN_EPS, ALU.add), deps=[t4])
                    t5 = p.emit("act", lambda e, i=i: e.activation(
                        out=s3[i][0:1, :], in_=s4[i][0:1, :], func=AF.Ln), deps=[t4b])
                    t6 = p.emit("act", lambda e, i=i: e.activation(
                        out=rrow[i][0:1, :], in_=s3[i][0:1, :], func=AF.Exp,
                        scale=-0.5), deps=[t5, t89p[0]])
                    t7 = p.emit("dve", lambda e, i=i: e.scalar_tensor_tensor(
                        s3[i][0:1, :], mrow[i][0:1, :], -1.0, rrow[i][0:1, :],
                        ALU.mult, ALU.mult), deps=[t6])
                    d6 = p.bank_take(bb0)
                    d7 = p.bank_take(bb1)
                    t8 = p.emit("pe", lambda e, i=i, bb0=bb0: e.matmul(
                        pb[bb0][:, :], ones_rf[0:1, :], rrow[i][0:1, :],
                        start=True, stop=True), deps=[t6] + d6)
                    t9 = p.emit("pe", lambda e, i=i, bb1=bb1: e.matmul(
                        pb[bb1][:, :], ones_rf[0:1, :], s3[i][0:1, :],
                        start=True, stop=True), deps=[t7] + d7)
                    t5_tok[tc] = t5
                    t7_tok[tc] = t7
                    t89_tok[tc] = (t8, t9)

                def stage_c(tc):
                    i = tc % 2
                    i3 = tc % 4
                    bb0, bb1 = BBANKS[i]
                    t8, t9 = t89_tok[tc]
                    last = scr_rel[0]
                    for c in range(DC):
                        a1 = p.emit("dve", lambda e, i3=i3, c=c, bb0=bb0: e.tensor_tensor(
                            scr[:, :], xc[i3][:, c * 512:(c + 1) * 512], pb[bb0][:, :],
                            ALU.mult), deps=[t8, ld_tok[i3], xnc_rel[0], last])
                        if sink_sbuf is not None:
                            out_ap = sink_sbuf[:, c * T + tc * 512: c * T + (tc + 1) * 512]
                            a2 = p.emit("dve", lambda e, o=out_ap, bb1=bb1: e.tensor_tensor(
                                o, scr[:, :], pb[bb1][:, :], ALU.add), deps=[t9, a1])
                        else:
                            a2 = p.emit("dve", lambda e, i=i, c=c, bb1=bb1: e.tensor_tensor(
                                xnc[i][:, c * 512:(c + 1) * 512], scr[:, :],
                                pb[bb1][:, :], ALU.add), deps=[t9, a1])
                        last = a2
                    p.bank_read(bb0, last)
                    p.bank_read(bb1, last)
                    scr_rel[0] = last
                    xc_rel[i3] = last
                    if sink_dram is not None:
                        dst = sink_dram.ap()[:, tc * DC * 512:(tc + 1) * DC * 512]
                        xnc_rel[0] = p.emit("sp", lambda e, i=i, d=dst: e.dma_start(
                            out=d, in_=xnc[i][:, :]), deps=[last], dma=True,
                            sem="lns")

                for k in range(nsteps):
                    if k == 0:
                        emit_ld(0)
                    if k + 1 < TC:
                        emit_ld(k + 1)
                    if k < TC:
                        stage_a(k)
                    if 1 <= k <= TC:
                        stage_b(k - 1)
                    if k >= 2:
                        stage_c(k - 2)
                    for _ in range(nun):
                        if units:
                            units.pop(0)()
                while units:
                    units.pop(0)()
                p.barrier()
                p.flush()

        s12 = ctx.enter_context(ExitStack())
        s35 = None
        s68 = None

        if upto >= 1:
            sw1T = sb("sw1T", [128, DC * H], BF16, s12)
            p1_stack = ctx.enter_context(ExitStack())
            w1c = [sb(f"w1c{i}", [128, D], F32, p1_stack) for i in range(2)]
            w1tc = [sb(f"w1tc{i}", [128, 2048], F32, p1_stack) for i in range(2)]
            a1raw = sb("a1raw", [128, HC], F32, p1_stack)
            p1_state = {"w1rel": [None, None], "sg1rel": [None, None], "red1": []}

            def w1a_load(hc, st=p1_state):
                i = hc % 2
                src = t_w1n.ap()[hc * 128:(hc + 1) * 128, :]
                p.emit("gp", lambda e, i=i, s=src: e.dma_start(
                    out=w1c[i][:, :], in_=s), deps=[st["w1rel"][0]], dma=True,
                    sem=f"w1a{i}")

            def w1_alpha(hc, st=p1_state):
                if hc == 0:
                    w1a_load(0)
                if hc + 1 < HC:
                    w1a_load(hc + 1)
                i = hc % 2
                rd = p.emit("dve", lambda e, i=i, hc=hc: e.reduce_sum(
                    a1raw[:, hc:hc + 1], w1c[i][:, :], mybir.AxisListType.X,
                    apply_absolute_value=True), deps=[p.last(f"w1a{i}")])
                st["w1rel"][0] = rd
                st["red1"].append(rd)

            def w1g_load(k, st=p1_state):
                c, half = k // 2, k % 2
                i = k % 2
                src = t_w1t.ap()[c * 128:(c + 1) * 128,
                                 half * 2048:(half + 1) * 2048]
                p.emit("gp", lambda e, i=i, s=src: e.dma_start(
                    out=w1tc[i][:, :], in_=s), deps=[st["sg1rel"][0]], dma=True,
                    sem=f"w1g{i}")

            def w1_sign(k, st=p1_state):
                if k == 0:
                    w1g_load(0)
                if k + 1 < 2 * DC:
                    w1g_load(k + 1)
                c, half = k // 2, k % 2
                i = k % 2
                sg = p.emit("act", lambda e, i=i, c=c, half=half: e.activation(
                    out=sw1T[:, c * H + half * 2048: c * H + (half + 1) * 2048],
                    in_=w1tc[i][:, :], func=AF.Sign), deps=[p.last(f"w1g{i}")])
                st["sg1rel"][0] = sg
                st["sgtok", k] = sg

            def alpha1_scale(st=p1_state):
                p.emit("dve", lambda e: e.tensor_single_scalar(
                    alpha1[:, :], a1raw[:, :], 1.0 / D, ALU.mult),
                    deps=st["red1"][-1:])

            p1_units = []
            for hc in range(HC):
                p1_units.append(lambda hc=hc: w1_alpha(hc))
                if hc % 2 == 0:
                    p1_units.append(lambda k=hc // 2: w1_sign(k))
            p1_units.append(alpha1_scale)
        else:
            p1_units = []

        if upto >= 0:
            ln_pass(t_xT, sink_dram=t_xnsp, units=p1_units if upto >= 1 else None)
        elif upto >= 1:
            for u in p1_units:
                u()
            p.barrier()
            p.flush()
        if upto >= 1:
            p1_stack.close()

        w2sg_last = None
        alpha2_tok = None
        if upto >= 1:
            sw2T = sb("sw2T", [128, HC * D], BF16, s12)
            w2c = [sb(f"w2c{i}", [128, H], BF16, s12) for i in range(2)]
            w2tc = [sb(f"w2tc{i}", [128, D], F32, s12) for i in range(2)]
            a2raw = sb("a2raw", [128, DC], F32, s12)
            w2rel = [None, None]
            sg2rel = [None, None]
            red2 = None
            def w2a_load(dc):
                i = dc % 2
                src = t_w2n.ap()[dc * 128:(dc + 1) * 128, :]
                p.emit("gp", lambda e, i=i, s=src: e.dma_start(
                    out=w2c[i][:, :], in_=s), deps=[w2rel[0]], dma=True,
                    sem=f"w2a{i}")

            w2a_load(0)
            for dc in range(DC):
                if dc + 1 < DC:
                    w2a_load(dc + 1)
                i = dc % 2
                red2 = p.emit("dve", lambda e, i=i, dc=dc: e.reduce_sum(
                    a2raw[:, dc:dc + 1], w2c[i][:, :], mybir.AxisListType.X,
                    apply_absolute_value=True), deps=[p.last(f"w2a{i}")])
                w2rel[0] = red2
            alpha2_tok = p.emit("dve", lambda e: e.tensor_single_scalar(
                alpha2[:, :], a2raw[:, :], 1.0 / H, ALU.mult), deps=[red2])

            def w2g_load(hc):
                i = hc % 2
                src = t_w2t.ap()[hc * 128:(hc + 1) * 128, :]
                p.emit("sp", lambda e, i=i, s=src: e.dma_start(
                    out=w2tc[i][:, :], in_=s), deps=[sg2rel[0]], dma=True,
                    sem=f"w2g{i}")

            def w2_sign(hc):
                if hc == 0:
                    w2g_load(0)
                if hc + 1 < HC:
                    w2g_load(hc + 1)
                i = hc % 2
                sg = p.emit("act", lambda e, i=i, hc=hc: e.activation(
                    out=sw2T[:, hc * D:(hc + 1) * D],
                    in_=w2tc[i][:, :], func=AF.Sign), deps=[p.last(f"w2g{i}")])
                sg2rel[0] = sg

            w2units = [lambda hc=hc: w2_sign(hc) for hc in range(HC)]

        if upto >= 2:
            with ExitStack() as st:
                xnck = [sb(f"f_xn{i}", [128, DC * 512], BF16, st) for i in range(2)]
                gTc = sb("gTc", [128, HC * 512], BF16, st)
                xrs = [sb(f"xrs{i}", [128, 512], F32, st) for i in range(2)]
                sc1 = sb("sc1", [128, 512], BF16, st)
                gfl = sb("gfl", [1, 1], BF16, st)
                pout = [sb("pout0", [128, 512], BF16, st)] * 2
                xn_rel = [None, None]
                xr_rel = [p.last("w2a0"), p.last("w2a1")]
                po_rel = [None, None]
                prev_B_last = None
                for tc in range(TC):
                    i = tc % 2
                    src = t_xnsp.ap()[:, tc * DC * 512:(tc + 1) * DC * 512]
                    p.emit("sp", lambda e, i=i, s=src: e.dma_start(
                        out=xnck[i][:, :], in_=s), deps=[xn_rel[i]], dma=True,
                        sem=f"p2x{i}")
                    ld = p.last(f"p2x{i}")
                    gel_tok = {}
                    for h in range(HC):
                        bank = h % 4
                        dd = p.bank_take(bank)
                        mmt = None
                        for c in range(DC):
                            mmt = p.emit("pe", lambda e, h=h, c=c, i=i, bank=bank: e.matmul(
                                pb[bank][:, :],
                                sw1T[:, c * H + h * 128: c * H + (h + 1) * 128],
                                xnck[i][:, c * 512:(c + 1) * 512],
                                start=(c == 0), stop=(c == DC - 1)),
                                deps=[ld] + dd, inc=(c == DC - 1))
                        gel_tok[h] = p.bank_read(bank, p.emit(
                            "act", lambda e, h=h, bank=bank: e.activation(
                                out=gTc[:, h * 512:(h + 1) * 512], in_=pb[bank][:, :],
                                func=AF.Gelu, scale=alpha1[:, h:h + 1],
                                bias=b1v[:, h:h + 1]),
                            deps=[mmt, prev_B_last]))
                        if w2units:
                            w2units.pop(0)()
                    xn_rel[i] = gel_tok[HC - 1]
                    gflush = p.emit("act", lambda e: e.activation(
                        out=gfl[0:1, 0:1], in_=gTc[0:1, 0:1], func=AF.Identity),
                        deps=[gel_tok[HC - 1]])
                    for half in range(2):
                        for dq4 in range(4):
                            dout = half * 4 + dq4
                            bank = 4 + dq4
                            dd = p.bank_take(bank)
                            mmt = None
                            for h in range(HC):
                                mmt = p.emit("pe", lambda e, h=h, dout=dout, bank=bank: e.matmul(
                                    pb[bank][:, :],
                                    sw2T[:, h * D + dout * 128: h * D + (dout + 1) * 128],
                                    gTc[:, h * 512:(h + 1) * 512],
                                    start=(h == 0), stop=(h == HC - 1)),
                                    deps=[gel_tok[h], gflush, sg2rel[0], sg2rel[1]]
                                    + (dd if h == 0 else []),
                                    inc=(h == HC - 1))
                            j = dout % 2
                            xsrc = t_xT.ap()[:, tc * DC * 512 + dout * 512:
                                             tc * DC * 512 + (dout + 1) * 512]
                            p.emit("gp", lambda e, j=j, s=xsrc: e.dma_start(
                                out=xrs[j][:, :], in_=s), deps=[xr_rel[j]],
                                dma=True, sem=f"p2r{j}")
                            lt = p.last(f"p2r{j}")
                            et = p.bank_read(bank, p.emit(
                                "act", lambda e, dout=dout, bank=bank: e.activation(
                                    out=sc1[:, :], in_=pb[bank][:, :], func=AF.Identity,
                                    scale=alpha2[:, dout:dout + 1],
                                    bias=b2v[:, dout:dout + 1]),
                                deps=[mmt, prev_B_last, alpha2_tok]))
                            at = p.emit("dve", lambda e, j=j: e.tensor_tensor(
                                pout[j][:, :], sc1[:, :], xrs[j][:, :], ALU.add),
                                deps=[et, lt, po_rel[0], po_rel[1]])
                            prev_B_last = at
                            xr_rel[j] = at
                            dstp = t_pT.ap()[dout * 128:(dout + 1) * 128,
                                             tc * 512:(tc + 1) * 512]
                            po_rel[j] = p.emit("sp", lambda e, j=j, d=dstp: e.dma_start(
                                out=d, in_=pout[j][:, :]), deps=[at], dma=True,
                                sem="p2o")
                p.barrier()
                p.flush()
        if upto >= 2:
            s12.close()

        if upto >= 3:
            s35 = ctx.enter_context(ExitStack())
            fT = sb("fT", [128, DC * T], BF16, s35)
            negM = sb("negM", [1, T], BF16, s35)
            with ExitStack() as st:
                W0b = sb("W0b", [128, DC * D], BF16, st)
                xnck = [sb(f"p3xn{i}", [128, DC * 512], BF16, st) for i in range(2)]
                w0src = hap(t_W0, 0, [[D, 128], [128 * D, DC], [1, D]])
                w0t = p.emit("gp", lambda e: e.dma_start(out=W0b[:, :], in_=w0src),
                             dma=True, sem="p3w")
                xn_rel = [None, None]
                for tc in range(TC):
                    i = tc % 2
                    src = t_xnsp.ap()[:, tc * DC * 512:(tc + 1) * DC * 512]
                    p.emit("sp", lambda e, i=i, s=src: e.dma_start(
                        out=xnck[i][:, :], in_=s), deps=[xn_rel[i]], dma=True,
                        sem=f"p3x{i}")
                    ld = p.last(f"p3x{i}")
                    last_ev = None
                    for dout in range(DC):
                        dd = p.bank_take(dout)
                        mmt = None
                        for c in range(DC):
                            mmt = p.emit("pe", lambda e, c=c, dout=dout, i=i: e.matmul(
                                pb[dout][:, :],
                                W0b[:, c * D + dout * 128: c * D + (dout + 1) * 128],
                                xnck[i][:, c * 512:(c + 1) * 512],
                                start=(c == 0), stop=(c == DC - 1)),
                                deps=[ld, w0t] + dd, inc=(c == DC - 1))
                        last_ev = p.bank_read(dout, p.emit(
                            "act", lambda e, dout=dout, tc=tc: e.activation(
                                out=fT[:, dout * T + tc * 512: dout * T + (tc + 1) * 512],
                                in_=pb[dout][:, :], func=AF.Identity,
                                bias=b0v[:, dout:dout + 1]), deps=[mmt]))
                    xn_rel[i] = last_ev
                p.barrier()
                p.flush()

        if upto >= 4:
            f_tok = sb("f_tok", [128, TT * D], BF16, s35)
            band = sb("band", [128, 1152], F32, s35)
            with ExitStack() as st:
                sq = [sb(f"sqf{i}", [128, T], BF16, st) for i in range(2)]
                tk = p.emit("gp", lambda e: e.iota(band[:, :], [[-1, 1152]], base=512,
                                                   channel_multiplier=1,
                                                   allow_small_or_imprecise_dtypes=True))
                tk = p.emit("act", lambda e: e.activation(
                    out=band[:, :], in_=band[:, :], func=AF.Abs), deps=[tk])
                tk = p.emit("dve", lambda e: e.tensor_single_scalar(
                    band[:, :], band[:, :], 10.0, ALU.min), deps=[tk])
                p.emit("dve", lambda e: e.tensor_scalar(
                    band[:, :], band[:, :], 0.01, -0.1, ALU.mult, ALU.add), deps=[tk])
                for c in range(DC):
                    dst = bass.AP(f_tok, c * 128, [[TT * D, 128], [D, TT], [1, 128]])
                    p.emit("sp", lambda e, d=dst, c=c: e.dma_start_transpose(
                        d, fT[:, c * T:(c + 1) * T]), dma=True, sem="p4t")
                mm_last = {}
                prev_sq = None
                for c in range(DC):
                    i = c % 2
                    sqt = p.emit("act", lambda e, c=c, i=i: e.activation(
                        out=sq[i][:, :], in_=fT[:, c * T:(c + 1) * T], func=AF.Square),
                        deps=[prev_sq])
                    for tc in range(TC):
                        dd = p.bank_take(tc) if c == 0 else []
                        tok = p.emit("pe", lambda e, c=c, i=i, tc=tc: e.matmul(
                            pb[tc][0:1, :], ones_col[:, :],
                            sq[i][:, tc * 512:(tc + 1) * 512],
                            start=(c == 0), stop=(c == DC - 1)),
                            deps=[sqt] + dd,
                            inc=(c == DC - 1) or (tc == TC - 1))
                        if c == DC - 1:
                            mm_last[tc] = tok
                    prev_sq = tok
                for tc in range(TC):
                    p.bank_read(tc, p.emit("dve", lambda e, tc=tc: e.tensor_single_scalar(
                        negM[0:1, tc * 512:(tc + 1) * 512], pb[tc][0:1, :], -1.0,
                        ALU.mult), deps=[mm_last[tc]]))
                p.barrier()
                p.flush()

        if upto >= 5:
            with ExitStack() as st:
                at = sb("atil", [128, TT * 512], BF16, st)
                zinv = sb("zinv", [1, 512], F32, st)
                flsc = sb("flsc", [1, 1], F32, st)
                zinvB = sb("zinvB", [128, 512], F32, st)
                negMB = sb("negMB", [128, 512], F32, st)
                pc = [sb(f"pchunk{i}", [128, 512], BF16, st) for i in range(2)]
                sc5 = [sb(f"sc5{i}", [128, 512], F32, st) for i in range(2)]
                x1o = [sb(f"x1o{i}", [128, 512], F32, st) for i in range(2)]
                pc_rel = [None, None]
                x1o_rel = [None, None]
                bandtok = None
                eb = [0, 1, 4, 5]
                nb_prev = None
                for g in range(TC):
                    exp_tok = {}
                    z_tok = None
                    dz = p.bank_take(2)
                    d3b = p.bank_take(3)
                    nb_mm = p.emit("pe", lambda e, g=g: e.matmul(
                        pb[3][:, :], ones_row[0:1, :],
                        negM[0:1, g * 512:(g + 1) * 512],
                        start=True, stop=True), deps=d3b)
                    nb_tok = p.bank_read(3, p.emit(
                        "dve", lambda e: e.tensor_copy(negMB[:, :], pb[3][:, :]),
                        deps=[nb_mm, nb_prev]))

                    def emit_z(tt, g=g, dz=dz):
                        return p.emit("pe", lambda e, tt=tt: e.matmul(
                            pb[2][0:1, :], ones_col[:, :],
                            at[:, tt * 512:(tt + 1) * 512],
                            start=(tt == 0), stop=(tt == TT - 1)),
                            deps=[exp_tok[tt]] + (dz if tt == 0 else []),
                            inc=(tt == TT - 1))

                    for tt in range(TT):
                        bank = eb[tt % 4]
                        dd = p.bank_take(bank)
                        mmt = None
                        for c in range(DC):
                            mmt = p.emit("pe", lambda e, c=c, tt=tt, g=g, bank=bank: e.matmul(
                                pb[bank][:, :],
                                fT[:, c * T + tt * 128: c * T + (tt + 1) * 128],
                                fT[:, c * T + g * 512: c * T + (g + 1) * 512],
                                start=(c == 0), stop=(c == DC - 1)),
                                deps=dd if c == 0 else [], inc=(c == DC - 1))
                        madd = p.emit("dve", lambda e, bank=bank: e.tensor_tensor(
                            pb[bank][:, :], pb[bank][:, :], negMB[:, :],
                            ALU.add), deps=[mmt, nb_tok])
                        nb_prev = madd
                        delta = 128 * tt - 512 * g
                        btok = None
                        if -128 <= delta <= 512:
                            btok = p.emit("dve", lambda e, d=delta, bank=bank: e.tensor_tensor(
                                pb[bank][:, :], pb[bank][:, :], band[:, 512 - d:1024 - d],
                                ALU.add), deps=[madd, bandtok])
                        et = p.bank_read(bank, p.emit(
                            "act", lambda e, tt=tt, bank=bank: e.activation(
                                out=at[:, tt * 512:(tt + 1) * 512], in_=pb[bank][:, :],
                                func=AF.Exp), deps=[btok if btok else madd]))
                        exp_tok[tt] = et
                        if tt >= 1:
                            z_tok = emit_z(tt - 1)
                    flush = p.emit("act", lambda e: e.activation(
                        out=flsc[0:1, 0:1], in_=at[0:1, 0:1], func=AF.Identity),
                        deps=[exp_tok[TT - 1]])
                    z_tok = p.emit("pe", lambda e: e.matmul(
                        pb[2][0:1, :], ones_col[:, :],
                        at[:, (TT - 1) * 512: TT * 512],
                        start=False, stop=True),
                        deps=[exp_tok[TT - 1], flush], inc=True)
                    zt = p.bank_read(2, p.emit(
                        "dve", lambda e: e.reciprocal(zinv[0:1, :], pb[2][0:1, :]),
                        deps=[z_tok]))
                    d3 = p.bank_take(3)
                    zb = p.emit("pe", lambda e: e.matmul(
                        pb[3][:, :], ones_rf[0:1, :], zinv[0:1, :],
                        start=True, stop=True), deps=[zt] + d3)
                    zc = p.bank_read(3, p.emit(
                        "dve", lambda e: e.tensor_copy(zinvB[:, :], pb[3][:, :]),
                        deps=[zb]))
                    for dout in range(DC):
                        bank = 4 + dout % 4
                        dd = p.bank_take(bank)
                        mmt = None
                        for tt in range(TT):
                            mmt = p.emit("pe", lambda e, tt=tt, dout=dout, bank=bank: e.matmul(
                                pb[bank][:, :],
                                f_tok[:, tt * D + dout * 128: tt * D + (dout + 1) * 128],
                                at[:, tt * 512:(tt + 1) * 512],
                                start=(tt == 0), stop=(tt == TT - 1)),
                                deps=[exp_tok[tt]] + (dd if tt == 0 else []),
                                inc=(tt == TT - 1))
                        j = dout % 2
                        psrc = t_pT.ap()[dout * 128:(dout + 1) * 128,
                                         g * 512:(g + 1) * 512]
                        p.emit("gp", lambda e, j=j, s=psrc: e.dma_start(
                            out=pc[j][:, :], in_=s), deps=[pc_rel[j]], dma=True,
                            sem=f"p5p{j}")
                        m1 = p.bank_read(bank, p.emit(
                            "dve", lambda e, j=j, bank=bank: e.tensor_tensor(
                                sc5[j][:, :], pb[bank][:, :], zinvB[:, :], ALU.mult),
                            deps=[mmt, zc]))
                        m2 = p.emit("dve", lambda e, j=j: e.tensor_tensor(
                            x1o[j][:, :], sc5[j][:, :], pc[j][:, :], ALU.add),
                            deps=[m1, p.last(f"p5p{j}"), x1o_rel[j]])
                        pc_rel[j] = m2
                        dst = t_x1T.ap()[:, g * DC * 512 + dout * 512:
                                         g * DC * 512 + (dout + 1) * 512]
                        x1o_rel[j] = p.emit("sp", lambda e, j=j, d=dst: e.dma_start(
                            out=d, in_=x1o[j][:, :]), deps=[m2], dma=True,
                            sem=f"p5o{j}")
                    p.barrier()
                p.flush()
        if upto >= 5 and s35 is not None:
            s35.close()

        if upto >= 6:
            s68 = ctx.enter_context(ExitStack())
            xn1T = sb("xn1T", [128, DC * T], BF16, s68)
            s67 = ctx.enter_context(ExitStack())
            binb = sb("binb", [1, WID], BF16, s67)
            p.emit("gp", lambda e: e.dma_start(out=binb[:, :], in_=t_binr.ap()),
                   dma=True, sem="p6b")
            ln_pass(t_x1T, sink_sbuf=xn1T)

        if upto >= 7:
            with ExitStack() as st:
                WinTb = sb("WinTb", [128, DC * WID], BF16, st)
                WoutTb = sb("WoutTb", [128, WK * D], BF16, st)
                silu_t = [sb(f"silu_t{i}", [128, WID], BF16, st) for i in range(2)]
                dec_t = [sb(f"dec_t{i}", [128, WID], BF16, st) for i in range(2)]
                mask_t = [sb(f"mask_t{i}", [128, WID], BF16, st) for i in range(2)]
                masked_t = [sb(f"masked_t{i}", [128, WID], BF16, st) for i in range(2)]
                mTc = [sb(f"mTc{i}", [128, WK * 512], BF16, st) for i in range(2)]
                xrs = [sb(f"p7xr{i}", [128, 512], F32, st) for i in range(2)]
                p1o = [sb(f"p1o{i}", [128, 512], BF16, st) for i in range(2)]
                wis = hap(t_WinT, 0, [[WID, 128], [128 * WID, DC], [1, WID]])
                p.emit("gp", lambda e: e.dma_start(out=WinTb[:, :], in_=wis),
                       dma=True, sem="p7w")
                wos = hap(t_WoutT, 0, [[D, 128], [128 * D, WK], [1, D]])
                p.emit("gp", lambda e: e.dma_start(out=WoutTb[:, :], in_=wos),
                       dma=True, sem="p7w")
                wt1 = wt2 = p.last("p7w")
                xr_rel = [wt1, wt1]
                po_rel = [None, None]
                mtc_rel = [None, None]
                tr_all = [None] * TC
                route_rel = [None, None]

                prev_tr = [None, None]

                def stage_a_tt(tcc, ii):
                    i = tcc % 2
                    if True:
                        tt = tcc * 4 + ii
                        i2 = tt % 2
                        sd_toks = []
                        for wc in range(WC):
                            bank = wc % 4
                            dd = p.bank_take(bank)
                            for c in range(DC):
                                p.emit("pe", lambda e, c=c, tt=tt, wc=wc, bank=bank: e.matmul(
                                    pb[bank][:, :],
                                    xn1T[:, c * T + tt * 128: c * T + (tt + 1) * 128],
                                    WinTb[:, c * WID + wc * 512: c * WID + (wc + 1) * 512],
                                    start=(c == 0), stop=False),
                                    deps=[wt1] + (dd if c == 0 else []), inc=False)
                            mmt = p.emit("pe", lambda e, wc=wc, bank=bank: e.matmul(
                                pb[bank][:, :], ones_row[0:1, :],
                                binb[0:1, wc * 512:(wc + 1) * 512],
                                start=False, stop=True), inc=True)
                            s1 = p.bank_read(bank, p.emit(
                                "act", lambda e, wc=wc, i2=i2, bank=bank: e.activation(
                                    out=silu_t[i2][:, wc * 512:(wc + 1) * 512],
                                    in_=pb[bank][:, :], func=AF.Silu),
                                deps=[mmt, route_rel[i2]]))
                            s2 = p.bank_read(bank, p.emit(
                                "dve", lambda e, wc=wc, i2=i2, bank=bank: e.tensor_single_scalar(
                                    dec_t[i2][:, wc * 512:(wc + 1) * 512], pb[bank][:, :],
                                    0.0, ALU.is_gt), deps=[mmt, route_rel[i2], s1]))
                            sd_toks += [s1, s2]
                        mt = p.emit("dve", lambda e, i2=i2: e.memset(
                            bass.AP(mask_t[i2], NNODE, [[WID, 128], [256, NTREE], [1, 1]]),
                            0.0), deps=sd_toks + [prev_tr[i2]])
                        mt = p.emit("dve", lambda e, i2=i2: e.memset(
                            bass.AP(mask_t[i2], 0, [[WID, 128], [256, NTREE], [1, 1]]),
                            1.0), deps=[mt])
                        for lev in range(7):
                            o = (1 << lev) - 1
                            o2 = (1 << (lev + 1)) - 1
                            cnt = 1 << lev
                            pv = bass.AP(mask_t[i2], o, [[WID, 128], [256, NTREE], [1, cnt]])
                            dv = bass.AP(dec_t[i2], o, [[WID, 128], [256, NTREE], [1, cnt]])
                            ov = bass.AP(mask_t[i2], o2 + cnt,
                                         [[WID, 128], [256, NTREE], [1, cnt]])
                            ev_ = bass.AP(mask_t[i2], o2,
                                          [[WID, 128], [256, NTREE], [1, cnt]])
                            mt = p.emit("dve", lambda e, pv=pv, dv=dv, ov=ov: e.tensor_tensor(
                                ov, pv, dv, ALU.mult), deps=[mt])
                            mt = p.emit("dve", lambda e, pv=pv, ov=ov, ev_=ev_: e.tensor_tensor(
                                ev_, pv, ov, ALU.subtract), deps=[mt])
                        mk = p.emit("dve", lambda e, i2=i2: e.tensor_tensor(
                            masked_t[i2][:, :], silu_t[i2][:, :], mask_t[i2][:, :],
                            ALU.mult), deps=[mt])
                        route_rel[i2] = mk
                        dstm = bass.AP(mTc[i], ii * 128,
                                       [[WK * 512, 128], [512, WK], [1, 128]])
                        trk = p.emit("sp", lambda e, d=dstm, i=i, i2=i2: e.dma_start_transpose(
                            d, masked_t[i2][:, :]),
                            deps=[mk, mtc_rel[i]] if ii == 0 else [mk], dma=True,
                            sem=f"p7t{i2}")
                        prev_tr[i2] = trk
                        if ii == 3:
                            tr_all[tcc] = [prev_tr[0], prev_tr[1]]

                def stage_b_d2(tcc, k):
                    i = tcc % 2
                    for dout in (2 * k, 2 * k + 1):
                        bank = 4 + dout % 4
                        dd = p.bank_take(bank)
                        mmt = None
                        for wc in range(WK):
                            mmt = p.emit("pe", lambda e, wc=wc, dout=dout, i=i, bank=bank: e.matmul(
                                pb[bank][:, :],
                                WoutTb[:, wc * D + dout * 128: wc * D + (dout + 1) * 128],
                                mTc[i][:, wc * 512:(wc + 1) * 512],
                                start=(wc == 0), stop=(wc == WK - 1)),
                                deps=[wt2] + tr_all[tcc] + (dd if wc == 0 else []),
                                inc=(wc == WK - 1))
                        j = dout % 2
                        xsrc = t_x1T.ap()[:, tcc * DC * 512 + dout * 512:
                                          tcc * DC * 512 + (dout + 1) * 512]
                        p.emit("gp", lambda e, j=j, s=xsrc: e.dma_start(
                            out=xrs[j][:, :], in_=s), deps=[xr_rel[j]], dma=True,
                            sem=f"p7r{j}")
                        at2 = p.bank_read(bank, p.emit(
                            "dve", lambda e, j=j, bank=bank: e.tensor_tensor(
                                p1o[j][:, :], pb[bank][:, :], xrs[j][:, :], ALU.add),
                            deps=[mmt, p.last(f"p7r{j}"), po_rel[j]]))
                        xr_rel[j] = at2
                        dst = t_p1T.ap()[dout * 128:(dout + 1) * 128,
                                         tcc * 512:(tcc + 1) * 512]
                        po_rel[j] = p.emit("sp", lambda e, j=j, d=dst: e.dma_start(
                            out=d, in_=p1o[j][:, :]), deps=[at2], dma=True,
                            sem=f"p7o{j}")
                        if dout == DC - 1:
                            mtc_rel[i] = mmt

                for tcc in range(TC):
                    for ii in range(4):
                        stage_a_tt(tcc, ii)
                        if tcc >= 1:
                            stage_b_d2(tcc - 1, ii)
                for k in range(4):
                    stage_b_d2(TC - 1, k)
                p.barrier()
                p.flush()
        if upto >= 7:
            s67.close()

        if upto >= 8:
            with ExitStack() as st:
                WvTb = sb("WvTb", [128, DC * D], BF16, st)
                WoTb = sb("WoTb", [128, DC * D], BF16, st)
                vsub = sb("vsub", [128, 1024], F32, st)
                fsc = sb("fsc", [128, DC], F32, st)
                zofft = sb("zofft", [128, 4 * DC], F32, st)
                cstk = sb("cstk", [128, 4], F32, st)
                carr = sb("carr", [128, 2 * DC], F32, st)
                tr = sb("tramp", [128, 1024], F32, st)
                s_q = [sb(f"s_q{i}", [128, 1024], F32, st) for i in range(2)]
                s_n = [sb(f"s_n{i}", [128, 1024], F32, st) for i in range(2)]
                s_f = [sb(f"s_f{i}", [128, 1024], F32, st) for i in range(2)]
                s_sin = [sb(f"s_sin{i}", [128, 1024], F32, st) for i in range(2)]
                s_cos = [sb(f"s_cos{i}", [128, 1024], F32, st) for i in range(2)]
                cA = [sb(f"cA{i}", [128, 1024], F32, st) for i in range(2)]
                cB = [sb(f"cB{i}", [128, 1024], F32, st) for i in range(2)]
                rp_a = sb("rp_a", [128, 1024], BF16, st)
                rp_b = sb("rp_b", [128, 1024], BF16, st)
                ret8 = [[sb(f"ret8_{pp}_{j}", [128, 1024], BF16, st)
                         for j in range(DC)] for pp in range(2)]
                p1c = [sb(f"p1c8_{i}", [128, 512], BF16, st) for i in range(4)]
                osb = [sb(f"osb8_{i}", [128, 512], F32, st) for i in range(2)]
                onesT = sb("onesT", [128, 1024], BF16, st)
                ot = p.emit("dve", lambda e: e.memset(onesT[:, :], 1.0))
                p.emit("dve", lambda e: e.memset(cstk[:, 0:1], MAGIC))
                p.emit("dve", lambda e: e.memset(cstk[:, 1:2], -MAGIC))
                ck = p.emit("dve", lambda e: e.memset(cstk[:, 2:3], 0.25))
                fk = p.emit("dve", lambda e: e.tensor_single_scalar(
                    fsc[:, :], freqv[:, :], INV2PI, ALU.mult))
                zkt = []
                for z in range(4):
                    zkt.append(p.emit("dve", lambda e, z=z: e.tensor_single_scalar(
                        zofft[:, z * DC:(z + 1) * DC], fsc[:, :],
                        1024.0 * z, ALU.mult), deps=[fk]))
                wvl = hap(t_WvT, 0, [[D, 128], [128 * D, DC], [1, D]])
                p.emit("gp", lambda e: e.dma_start(out=WvTb[:, :], in_=wvl),
                       dma=True, sem="p8w")
                wol = hap(t_WoT, 0, [[D, 128], [128 * D, DC], [1, D]])
                p.emit("gp", lambda e: e.dma_start(out=WoTb[:, :], in_=wol),
                       dma=True, sem="p8w")
                trt = p.emit("gp", lambda e: e.iota(
                    tr[:, :], [[1, 1024]], base=0, channel_multiplier=0,
                    allow_small_or_imprecise_dtypes=True))
                wvt = wot = p.last("p8w")
                prev = {}
                ret_tok = [[None] * DC, [None] * DC]
                ret_rel = [None, None]
                pend_ret = [None]
                p1_rel = [None] * 4
                os_rel = [None, None]
                pk = [0, 0]
                kk_ = [0]

                st8 = {}

                def chainA(ts_, j):
                    g = kk_[0] % 2
                    kk_[0] += 1
                    mm_v = []
                    for h2 in range(2):
                        tc = ts_ * 2 + h2
                        bank = kk_[0] % 2 * 2 + h2
                        dd = p.bank_take(bank)
                        mmt = None
                        for c in range(DC):
                            mmt = p.emit("pe", lambda e, c=c, j=j, tc=tc, bank=bank: e.matmul(
                                pb[bank][:, :],
                                WvTb[:, c * D + j * 128: c * D + (j + 1) * 128],
                                xn1T[:, c * T + tc * 512: c * T + (tc + 1) * 512],
                                start=(c == 0), stop=(c == DC - 1)),
                                deps=[wvt] + (dd if c == 0 else []),
                                inc=(c == DC - 1))
                        mm_v.append((bank, mmt))
                    t1 = p.emit("act", lambda e, g=g, j=j, z=ts_: e.activation(
                        out=s_q[g][:, :], in_=tr[:, :], func=AF.Identity,
                        scale=fsc[:, j:j + 1],
                        bias=zofft[:, z * DC + j: z * DC + j + 1]),
                        deps=[trt, fk, zkt[ts_], prev.get(("q", g))])
                    t2a = p.emit("act", lambda e, g=g: e.activation(
                        out=s_n[g][:, :], in_=s_q[g][:, :], func=AF.Identity,
                        bias=cstk[:, 0:1]), deps=[t1, ck, prev.get(("n", g))])
                    t2b = p.emit("act", lambda e, g=g: e.activation(
                        out=s_n[g][:, :], in_=s_n[g][:, :], func=AF.Identity,
                        bias=cstk[:, 1:2]), deps=[t2a])
                    t3 = p.emit("dve", lambda e, g=g: e.tensor_tensor(
                        s_f[g][:, :], s_q[g][:, :], s_n[g][:, :], ALU.subtract),
                        deps=[t2b, prev.get(("f", g))])
                    st8[(ts_, j)] = (g, mm_v, t3)

                def chainB(ts_, j):
                    g, mm_v, t3 = st8.pop((ts_, j))
                    rp = ts_ % 2
                    if pend_ret[0] is not None:
                        fn = pend_ret[0]
                        pend_ret[0] = None
                        fn()
                    evs = []
                    for h2, (bank, mmt) in enumerate(mm_v):
                        evs.append(p.bank_read(bank, p.emit(
                            "act", lambda e, h2=h2, bank=bank: e.activation(
                                out=vsub[:, h2 * 512:(h2 + 1) * 512],
                                in_=pb[bank][:, :], func=AF.Identity),
                            deps=[mmt, prev.get("vsub")])))
                    sn = p.emit("act", lambda e, g=g: e.activation(
                        out=s_sin[g][:, :], in_=s_f[g][:, :], func=AF.Sin,
                        scale=TWOPI), deps=[t3, prev.get(("sin", g))])
                    t4 = p.emit("act", lambda e, g=g: e.activation(
                        out=s_q[g][:, :], in_=s_f[g][:, :], func=AF.Identity,
                        bias=cstk[:, 2:3]), deps=[t3])
                    t5a = p.emit("act", lambda e, g=g: e.activation(
                        out=s_n[g][:, :], in_=s_q[g][:, :], func=AF.Identity,
                        bias=cstk[:, 0:1]), deps=[t4, t3])
                    t5b = p.emit("act", lambda e, g=g: e.activation(
                        out=s_n[g][:, :], in_=s_n[g][:, :], func=AF.Identity,
                        bias=cstk[:, 1:2]), deps=[t5a])
                    t6 = p.emit("dve", lambda e, g=g: e.tensor_tensor(
                        s_f[g][:, :], s_q[g][:, :], s_n[g][:, :], ALU.subtract),
                        deps=[t5b, sn])
                    cs = p.emit("act", lambda e, g=g: e.activation(
                        out=s_cos[g][:, :], in_=s_f[g][:, :], func=AF.Sin,
                        scale=TWOPI), deps=[t6, prev.get(("cos", g))])
                    vr1 = p.emit("dve", lambda e, g=g: e.tensor_tensor(
                        s_q[g][:, :], vsub[:, :], s_cos[g][:, :], ALU.mult),
                        deps=[cs, t6] + evs)
                    vr2 = p.emit("dve", lambda e, g=g: e.tensor_tensor(
                        s_n[g][:, :], vsub[:, :], s_sin[g][:, :], ALU.mult),
                        deps=[sn, t6, vr1])
                    prev["vsub"] = vr2
                    ia = 0.0 if ts_ == 0 else carr[:, 2 * j: 2 * j + 1]
                    ib = 0.0 if ts_ == 0 else carr[:, 2 * j + 1: 2 * j + 2]
                    sc_a = p.emit("dve", lambda e, k=j % 2, g=g, ia=ia: e.tensor_tensor_scan(
                        cA[k][:, :], onesT[:, :], s_q[g][:, :], ia, ALU.mult,
                        ALU.add),
                        deps=[vr1, ot, prev.get(("car", j)), prev.get(("cA", j % 2))])
                    sc_b = p.emit("dve", lambda e, k=j % 2, g=g, ib=ib: e.tensor_tensor_scan(
                        cB[k][:, :], onesT[:, :], s_n[g][:, :], ib, ALU.mult,
                        ALU.add), deps=[vr2, prev.get(("cB", j % 2))])
                    ca = p.emit("dve", lambda e, k=j % 2, j=j: e.tensor_copy(
                        carr[:, 2 * j: 2 * j + 1], cA[k][:, 1023:1024]),
                        deps=[sc_a])
                    cb = p.emit("dve", lambda e, k=j % 2, j=j: e.tensor_copy(
                        carr[:, 2 * j + 1: 2 * j + 2], cB[k][:, 1023:1024]),
                        deps=[sc_b])
                    prev[("car", j)] = cb
                    r1 = p.emit("gp", lambda e, k=j % 2, g=g: e.tensor_tensor(
                        rp_a[:, :], s_cos[g][:, :], cA[k][:, :], ALU.mult),
                        deps=[sc_a, ca, prev.get("rp")])
                    r2 = p.emit("gp", lambda e, k=j % 2, g=g: e.tensor_tensor(
                        rp_b[:, :], s_sin[g][:, :], cB[k][:, :], ALU.mult),
                        deps=[sc_b, cb])

                    def emit_ret(rp=rp, j=j, r1=r1, r2=r2):
                        ret = p.emit("dve", lambda e, rp=rp, j=j: e.tensor_tensor(
                            ret8[rp][j][:, :], rp_a[:, :], rp_b[:, :], ALU.add),
                            deps=[r1, r2, ret_rel[rp]])
                        ret_tok[rp][j] = ret
                        prev["rp"] = ret
                    pend_ret[0] = emit_ret
                    prev[("q", g)] = sc_a
                    prev[("n", g)] = sc_b
                    prev[("f", g)] = cs
                    prev[("sin", g)] = r2
                    prev[("cos", g)] = r1
                    prev[("cA", j % 2)] = r1
                    prev[("cB", j % 2)] = r2

                def wave(ts_, wv):
                    rp = ts_ % 2
                    half, grp = wv // 2, wv % 2
                    co = ts_ * 1024 + half * 512
                    lts = []
                    for dd_ in range(4):
                        dout = grp * 4 + dd_
                        i4 = pk[0] % 4
                        pk[0] += 1
                        psrc = t_p1T.ap()[dout * 128:(dout + 1) * 128, co:co + 512]
                        lts.append((i4, p.emit("sp", lambda e, i4=i4, s=psrc: e.dma_start(
                            out=p1c[i4][:, :], in_=s), deps=[p1_rel[i4]],
                            dma=True, sem=f"p8p{i4}")))
                    mms = []
                    for dd_ in range(4):
                        dout = grp * 4 + dd_
                        bank = 4 + dd_
                        dbt = p.bank_take(bank)
                        mmt = None
                        for j in range(DC):
                            mmt = p.emit("pe", lambda e, j=j, dout=dout, half=half, bank=bank, rp=rp: e.matmul(
                                pb[bank][:, :],
                                WoTb[:, j * D + dout * 128: j * D + (dout + 1) * 128],
                                ret8[rp][j][:, half * 512:(half + 1) * 512],
                                start=(j == 0), stop=(j == DC - 1)),
                                deps=[wot, ret_tok[rp][j]] + (dbt if j == 0 else []),
                                inc=(j == DC - 1))
                        mms.append((bank, mmt))
                    ld_all = [p.last(f"p8p{k}") for k in range(4)]
                    for dd_ in range(4):
                        dout = grp * 4 + dd_
                        bank, mmt = mms[dd_]
                        i4, lt = lts[dd_]
                        oi = pk[1] % 2
                        pk[1] += 1
                        at3 = p.bank_read(bank, p.emit(
                            "dve", lambda e, i4=i4, oi=oi, bank=bank: e.tensor_tensor(
                                osb[oi][:, :], pb[bank][:, :], p1c[i4][:, :],
                                ALU.add), deps=[mmt, os_rel[oi]] + ld_all))
                        p1_rel[i4] = at3
                        dst = t_out.ap()[dout * 128:(dout + 1) * 128, co:co + 512]
                        os_rel[oi] = p.emit("sp", lambda e, oi=oi, d=dst: e.dma_start(
                            out=d, in_=osb[oi][:, :]), deps=[at3], dma=True,
                            sem="p8o")
                    if wv == 3:
                        ret_rel[rp] = mms[-1][1]

                seq = [(ts_, j) for ts_ in range(4) for j in range(DC)]
                for idx, (ts_, j) in enumerate(seq):
                    chainA(ts_, j)
                    if idx >= 1:
                        pt, pj = seq[idx - 1]
                        chainB(pt, pj)
                        if pt >= 1 and pj % 2 == 1:
                            wave(pt - 1, pj // 2)
                chainB(3, DC - 1)
                if pend_ret[0] is not None:
                    fn = pend_ret[0]
                    pend_ret[0] = None
                    fn()
                wave(2, 3)
                for wv in range(4):
                    wave(3, wv)
                p.barrier()
                p.flush()
        if upto >= 8 and s68 is not None:
            s68.close()

        p.flush()

    return nc


def prep_inputs(inputs):
    f = np.float32
    x = np.asarray(inputs["L0_attn_x"], f)
    W0 = np.ascontiguousarray(np.asarray(inputs["L0_attn_W"], f))
    b0 = np.asarray(inputs["L0_attn_b"], f)
    w1 = np.asarray(inputs["L0_ffn_w1"], f)
    b1 = np.asarray(inputs["L0_ffn_b1"], f)
    w2 = np.asarray(inputs["L0_ffn_w2"], f)
    b2 = np.asarray(inputs["L0_ffn_b2"], f)
    Wv = np.asarray(inputs["L1_attn_Wv"], f)
    Wo = np.asarray(inputs["L1_attn_Wo"], f)
    freqs = np.asarray(inputs["L1_attn_freqs"], f)
    Win = np.asarray(inputs["L1_ffn_W_in"], f)
    bin_ = np.asarray(inputs["L1_ffn_b_in"], f)
    Wout = np.asarray(inputs["L1_ffn_W_out"], f)

    shared = {
        "W0": W0,
        "b0v": np.ascontiguousarray(b0.reshape(DC, 128).T),
        "w1n": np.ascontiguousarray(w1),
        "w1t": np.ascontiguousarray(w1.T),
        "b1v": np.ascontiguousarray(b1.reshape(HC, 128).T),
        "w2n": np.ascontiguousarray(w2),
        "w2t": np.ascontiguousarray(w2.T),
        "b2v": np.ascontiguousarray(b2.reshape(DC, 128).T),
        "WvT": np.ascontiguousarray(Wv.T),
        "WoT": np.ascontiguousarray(Wo.T),
        "freqv": np.ascontiguousarray(freqs.reshape(DC, 128).T),
    }
    WinP = np.zeros((WID, D), f)
    binP = np.zeros((1, WID), f)
    WoutP = np.zeros((D, WID), f)
    wr = Win.reshape(NTREE, NNODE, D)
    br = bin_.reshape(NTREE, NNODE)
    wo = Wout.reshape(D, NTREE, NNODE)
    node_at = np.zeros(NNODE, np.int64)
    curj = [0]
    for lev in range(7):
        nxt = [2 * j for j in curj] + [2 * j + 1 for j in curj]
        o2 = (1 << (lev + 1)) - 1
        for k, j in enumerate(nxt):
            node_at[o2 + k] = o2 + j
        curj = nxt
    for t in range(NTREE):
        WinP[t * 256: t * 256 + NNODE, :] = wr[t][node_at]
        binP[0, t * 256: t * 256 + NNODE] = br[t][node_at]
        WoutP[:, t * 256: t * 256 + NNODE] = wo[:, t, :][:, node_at]
    shared["WinT"] = np.ascontiguousarray(WinP.T)
    shared["binr"] = binP
    shared["WoutT"] = np.ascontiguousarray(WoutP.T)

    in_maps = []
    for b in range(8):
        m = dict(shared)
        m["xT"] = np.ascontiguousarray(
            x[b].reshape(TC, 512, DC, 128).transpose(3, 0, 2, 1).reshape(128, TC * DC * 512))
        in_maps.append(m)
    return in_maps


def kernel(**inputs):
    in_maps = prep_inputs(inputs)
    nc = build_program(debug=False, upto=99)
    res = run_bass_kernel_spmd(nc, in_maps, core_ids=list(range(8)))
    out = np.stack([np.asarray(r["out"], np.float32).T for r in res.results], axis=0)
    return out


# version 40 (final)
# speedup vs baseline: 1.1053x; 1.0095x over previous
import numpy as np
from contextlib import ExitStack

import concourse.bass as bass
import concourse.mybir as mybir
from concourse.bass_utils import run_bass_kernel_spmd

F32 = mybir.dt.float32
BF16 = mybir.dt.bfloat16
AF = mybir.ActivationFunctionType
ALU = mybir.AluOpType

D = 1024
T = 4096
H = 4096
WID = 2048
NTREE = 8
NNODE = 255
DC = D // 128
TC = T // 512
TT = T // 128
HC = H // 128
WC = WID // 512
WK = WID // 128

MAGIC = 12582912.0
INV2PI = float(np.float32(1.0 / (2.0 * np.pi)))
CW1 = 6.28125
CW2 = float(np.float32(2.0 * np.pi - 6.28125))
PI = float(np.float32(np.pi))
HALFPI = float(np.float32(np.pi / 2.0))
TWOPI = float(np.float32(2.0 * np.pi))
LN_EPS = 1e-5


class Prog:

    ENGS = ("pe", "act", "dve", "sp", "gp")

    def __init__(self, nc, ctx):
        self.nc = nc
        self.ctx = ctx
        self.sem = {e: ctx.enter_context(nc.semaphore("sem_" + e)) for e in self.ENGS}
        self.cnt = {e: 0 for e in self.ENGS}
        self.wm = {e: {} for e in self.ENGS}
        self.q = {e: [] for e in self.ENGS}
        self.bankrel = {}

    def emit(self, eng, fn, deps=(), inc=True, dma=False, sem=None):
        key = sem if sem is not None else eng
        if key not in self.sem:
            self.sem[key] = self.ctx.enter_context(
                self.nc.semaphore("sem_" + key))
            self.cnt[key] = 0
        waits = []
        for d in deps:
            if not d:
                continue
            s, v = d
            if v > self.wm[eng].get(s, 0):
                waits.append((s, v))
                self.wm[eng][s] = v
        step = 16 if dma else 1
        tok = None
        if inc:
            self.cnt[key] += step
            tok = (key, self.cnt[key])
        semmap = self.sem
        myname = key

        def closure(engine_obj, waits=tuple(waits), fn=fn, tok=tok, step=step):
            for (s, v) in waits:
                engine_obj.wait_ge(semmap[s], v)
            inst = fn(engine_obj)
            if tok is not None:
                inst.then_inc(semmap[myname], step)

        self.q[eng].append(closure)
        return tok

    def last(self, eng):
        return (eng, self.cnt[eng]) if self.cnt.get(eng, 0) > 0 else None

    def bank_take(self, bid):
        return self.bankrel.pop(bid, [])

    def bank_read(self, bid, tok):
        self.bankrel.setdefault(bid, []).append(tok)
        return tok

    def barrier(self):
        toks = [(k, self.cnt[k]) for k in self.cnt if self.cnt[k] > 0]
        semmap = self.sem
        for e in self.ENGS:
            def closure(engine_obj, toks=tuple(toks), e=e):
                for (s, v) in toks:
                    engine_obj.wait_ge(semmap[s], v)
            self.q[e].append(closure)
            for (s, v) in toks:
                if v > self.wm[e].get(s, 0):
                    self.wm[e][s] = v
        self.bankrel = {}
        return toks

    def flush(self):
        nc = self.nc
        qs = {e: self.q[e] for e in self.ENGS}
        self.q = {e: [] for e in self.ENGS}
        if not any(qs.values()):
            return
        with nc.Block() as block:
            @block.tensor
            def _(e, fns=qs["pe"]):
                for f in fns:
                    f(e)

            @block.scalar
            def _(e, fns=qs["act"]):
                for f in fns:
                    f(e)

            @block.vector
            def _(e, fns=qs["dve"]):
                for f in fns:
                    f(e)

            @block.sync
            def _(e, fns=qs["sp"]):
                for f in fns:
                    f(e)

            @block.gpsimd
            def _(e, fns=qs["gp"]):
                for f in fns:
                    f(e)


def hap(t, off, dims):
    return bass.AP(t, off, [list(d) for d in dims])


P7CUT = 99


def build_program(debug=False, upto=99):
    nc = bass.Bass("TRN2", target_bir_lowering=False)

    t_xT = nc.dram_tensor("xT", [128, TC * DC * 512], F32, kind="ExternalInput")
    t_W0 = nc.dram_tensor("W0", [D, D], F32, kind="ExternalInput")
    t_b0v = nc.dram_tensor("b0v", [128, DC], F32, kind="ExternalInput")
    t_w1n = nc.dram_tensor("w1n", [H, D], F32, kind="ExternalInput")
    t_w1t = nc.dram_tensor("w1t", [D, H], F32, kind="ExternalInput")
    t_b1v = nc.dram_tensor("b1v", [128, HC], F32, kind="ExternalInput")
    t_w2n = nc.dram_tensor("w2n", [D, H], F32, kind="ExternalInput")
    t_w2t = nc.dram_tensor("w2t", [H, D], F32, kind="ExternalInput")
    t_b2v = nc.dram_tensor("b2v", [128, DC], F32, kind="ExternalInput")
    t_WvT = nc.dram_tensor("WvT", [D, D], F32, kind="ExternalInput")
    t_WoT = nc.dram_tensor("WoT", [D, D], F32, kind="ExternalInput")
    t_freqv = nc.dram_tensor("freqv", [128, DC], F32, kind="ExternalInput")
    t_WinT = nc.dram_tensor("WinT", [D, WID], F32, kind="ExternalInput")
    t_binr = nc.dram_tensor("binr", [1, WID], F32, kind="ExternalInput")
    t_WoutT = nc.dram_tensor("WoutT", [WID, D], F32, kind="ExternalInput")
    t_out = nc.dram_tensor("out", [D, T], F32, kind="ExternalOutput")

    dbg = "ExternalOutput" if debug else "Internal"
    t_xnsp = nc.dram_tensor("xnsp", [128, TC * DC * 512], BF16, kind=dbg)
    t_pT = nc.dram_tensor("pT", [D, T], BF16, kind=dbg)
    t_x1T = nc.dram_tensor("x1T", [128, TC * DC * 512], F32, kind=dbg)
    t_p1T = nc.dram_tensor("p1T", [D, T], BF16, kind=dbg)

    with ExitStack() as ctx:
        p = Prog(nc, ctx)

        _sbn = [0]

        def sb(name, shape, dt, st=None, side=None):
            _sbn[0] += 1
            kw = {"side": side} if side else {}
            return (st or ctx).enter_context(
                nc.sbuf_tensor(f"sb{_sbn[0]}_{name}", shape, dt, **kw))

        pb = [ctx.enter_context(nc.psum_tensor(f"pb{i}", [128, 512], F32))
              for i in range(8)]

        ones_col = sb("ones_col", [128, 1], BF16)
        ones_row = sb("ones_row", [1, 128], BF16)
        ones_rf = sb("ones_rf", [1, 128], F32)
        b0v = sb("b0v", [128, DC], F32)
        b1v = sb("b1v", [128, HC], F32)
        b2v = sb("b2v", [128, DC], F32)
        freqv = sb("freqv", [128, DC], F32)
        alpha1 = sb("alpha1", [128, HC], F32)
        alpha2 = sb("alpha2", [128, DC], F32)

        p.emit("dve", lambda e: e.memset(ones_col[:, :], 1.0))
        p.emit("dve", lambda e: e.memset(ones_row[:, :], 1.0))
        p.emit("dve", lambda e: e.memset(ones_rf[:, :], 1.0))
        p.emit("sp", lambda e: e.dma_start(out=b0v[:, :], in_=t_b0v.ap()), dma=True)
        p.emit("sp", lambda e: e.dma_start(out=b1v[:, :], in_=t_b1v.ap()), dma=True)
        p.emit("sp", lambda e: e.dma_start(out=b2v[:, :], in_=t_b2v.ap()), dma=True)
        p.emit("sp", lambda e: e.dma_start(out=freqv[:, :], in_=t_freqv.ap()), dma=True)
        p.barrier()

        def ln_pass(src_dram, sink_dram=None, sink_sbuf=None, units=None):
            units = list(units or [])
            nsteps = TC + 2
            nun = max(1, (len(units) + nsteps - 1) // nsteps) if units else 0
            with ExitStack() as st:
                xc = [sb(f"ln_xc{i}", [128, DC * 512], F32, st) for i in range(4)]
                xcb = [sb(f"ln_xcb{i}", [128, T], BF16, st) for i in range(2)]
                xsq = [sb("ln_xsq0", [128, T], BF16, st)] * 2
                scr = sb("ln_scr", [128, 512], F32, st)
                mrow = [sb(f"ln_mrow{i}", [1, 512], F32, st) for i in range(2)]
                rrow = [sb(f"ln_rrow{i}", [1, 512], F32, st) for i in range(2)]
                s3 = [sb(f"ln_s3{i}", [1, 512], F32, st) for i in range(2)]
                s4 = [sb(f"ln_s4{i}", [1, 512], F32, st) for i in range(2)]
                xnc = [sb("ln_xnc0", [128, DC * 512], BF16, st)] * 2
                ld_tok = [None] * 4
                xc_rel = [None] * 4
                scr_rel = [None]
                xnc_rel = [None]
                t12_tok = {}
                mm_last = {}
                t5_tok = {}
                t7_tok = {}
                t89_tok = {}
                SBANKS = [(4, 5), (2, 3)]
                BBANKS = [(6, 7), (0, 1)]
                ldeng = "gp" if src_dram.dtype == BF16 else "sp"

                def emit_ld(tc):
                    i3 = tc % 4
                    src = src_dram.ap()[:, tc * DC * 512:(tc + 1) * DC * 512]
                    p.emit(
                        ldeng, lambda e, o=xc[i3], s=src: e.dma_start(out=o[:, :], in_=s),
                        deps=[xc_rel[i3]], dma=True, sem=f"lnx{i3}")
                    ld_tok[i3] = p.last(f"lnx{i3}")

                def stage_a(tc):
                    i = tc % 2
                    i3 = tc % 4
                    sb0, sb1 = SBANKS[i]
                    ct = p.emit("act", lambda e, i=i, i3=i3: e.activation(
                        out=xcb[i][:, :], in_=xc[i3][:, :], func=AF.Identity),
                        deps=[ld_tok[i3], mm_last.get(tc - 2)])
                    sq = p.emit("dve", lambda e, i=i: e.tensor_tensor(
                        xsq[i][:, :], xcb[i][:, :], xcb[i][:, :], ALU.mult),
                        deps=[ct, mm_last.get(tc - 1)])
                    d4 = p.bank_take(sb0)
                    d5 = p.bank_take(sb1)
                    for c in range(DC):
                        p.emit("pe", lambda e, c=c, i=i, sb0=sb0: e.matmul(
                            pb[sb0][0:1, :], ones_col[:, :],
                            xcb[i][:, c * 512:(c + 1) * 512],
                            start=(c == 0), stop=(c == DC - 1)),
                            deps=[ct] + d4, inc=False)
                    mm_tok = None
                    for c in range(DC):
                        mm_tok = p.emit("pe", lambda e, c=c, i=i, sb1=sb1: e.matmul(
                            pb[sb1][0:1, :], ones_col[:, :],
                            xsq[i][:, c * 512:(c + 1) * 512],
                            start=(c == 0), stop=(c == DC - 1)),
                            deps=[sq] + d5, inc=(c == DC - 1))
                    mm_last[tc] = mm_tok
                    t89p = t89_tok.get(tc - 2, (None, None))
                    t1 = p.bank_read(sb0, p.emit("dve", lambda e, i=i, sb0=sb0: e.tensor_single_scalar(
                        mrow[i][0:1, :], pb[sb0][0:1, :], 1.0 / D, ALU.mult),
                        deps=[mm_tok, t7_tok.get(tc - 2)]))
                    t2 = p.bank_read(sb1, p.emit("dve", lambda e, i=i, sb1=sb1: e.tensor_single_scalar(
                        s3[i][0:1, :], pb[sb1][0:1, :], 1.0 / D, ALU.mult),
                        deps=[mm_tok, t89p[1]]))
                    t12_tok[tc] = (t1, t2)

                def stage_b(tc):
                    i = tc % 2
                    bb0, bb1 = BBANKS[i]
                    t1, t2 = t12_tok.pop(tc)
                    t89p = t89_tok.get(tc - 2, (None, None))
                    t3 = p.emit("dve", lambda e, i=i: e.tensor_tensor(
                        s4[i][0:1, :], mrow[i][0:1, :], mrow[i][0:1, :], ALU.mult),
                        deps=[t1, t5_tok.get(tc - 2)])
                    t4 = p.emit("dve", lambda e, i=i: e.tensor_tensor(
                        s3[i][0:1, :], s3[i][0:1, :], s4[i][0:1, :], ALU.subtract),
                        deps=[t2, t3])
                    t4b = p.emit("dve", lambda e, i=i: e.tensor_single_scalar(
                        s4[i][0:1, :], s3[i][0:1, :], LN_EPS, ALU.add), deps=[t4])
                    t5 = p.emit("act", lambda e, i=i: e.activation(
                        out=s3[i][0:1, :], in_=s4[i][0:1, :], func=AF.Ln), deps=[t4b])
                    t6 = p.emit("act", lambda e, i=i: e.activation(
                        out=rrow[i][0:1, :], in_=s3[i][0:1, :], func=AF.Exp,
                        scale=-0.5), deps=[t5, t89p[0]])
                    t7 = p.emit("dve", lambda e, i=i: e.scalar_tensor_tensor(
                        s3[i][0:1, :], mrow[i][0:1, :], -1.0, rrow[i][0:1, :],
                        ALU.mult, ALU.mult), deps=[t6])
                    d6 = p.bank_take(bb0)
                    d7 = p.bank_take(bb1)
                    t8 = p.emit("pe", lambda e, i=i, bb0=bb0: e.matmul(
                        pb[bb0][:, :], ones_rf[0:1, :], rrow[i][0:1, :],
                        start=True, stop=True), deps=[t6] + d6)
                    t9 = p.emit("pe", lambda e, i=i, bb1=bb1: e.matmul(
                        pb[bb1][:, :], ones_rf[0:1, :], s3[i][0:1, :],
                        start=True, stop=True), deps=[t7] + d7)
                    t5_tok[tc] = t5
                    t7_tok[tc] = t7
                    t89_tok[tc] = (t8, t9)

                def stage_c(tc):
                    i = tc % 2
                    i3 = tc % 4
                    bb0, bb1 = BBANKS[i]
                    t8, t9 = t89_tok[tc]
                    last = scr_rel[0]
                    for c in range(DC):
                        a1 = p.emit("dve", lambda e, i3=i3, c=c, bb0=bb0: e.tensor_tensor(
                            scr[:, :], xc[i3][:, c * 512:(c + 1) * 512], pb[bb0][:, :],
                            ALU.mult), deps=[t8, ld_tok[i3], xnc_rel[0], last])
                        if sink_sbuf is not None:
                            out_ap = sink_sbuf[:, c * T + tc * 512: c * T + (tc + 1) * 512]
                            a2 = p.emit("dve", lambda e, o=out_ap, bb1=bb1: e.tensor_tensor(
                                o, scr[:, :], pb[bb1][:, :], ALU.add), deps=[t9, a1])
                        else:
                            a2 = p.emit("dve", lambda e, i=i, c=c, bb1=bb1: e.tensor_tensor(
                                xnc[i][:, c * 512:(c + 1) * 512], scr[:, :],
                                pb[bb1][:, :], ALU.add), deps=[t9, a1])
                        last = a2
                    p.bank_read(bb0, last)
                    p.bank_read(bb1, last)
                    scr_rel[0] = last
                    xc_rel[i3] = last
                    if sink_dram is not None:
                        dst = sink_dram.ap()[:, tc * DC * 512:(tc + 1) * DC * 512]
                        xnc_rel[0] = p.emit("sp", lambda e, i=i, d=dst: e.dma_start(
                            out=d, in_=xnc[i][:, :]), deps=[last], dma=True,
                            sem="lns")

                for k in range(nsteps):
                    if k == 0:
                        emit_ld(0)
                    if k + 1 < TC:
                        emit_ld(k + 1)
                    if k < TC:
                        stage_a(k)
                    if 1 <= k <= TC:
                        stage_b(k - 1)
                    if k >= 2:
                        stage_c(k - 2)
                    for _ in range(nun):
                        if units:
                            units.pop(0)()
                while units:
                    units.pop(0)()
                p.barrier()
                p.flush()

        s12 = ctx.enter_context(ExitStack())
        s35 = None
        s68 = None

        if upto >= 1:
            sw1T = sb("sw1T", [128, DC * H], BF16, s12)
            p1_stack = ctx.enter_context(ExitStack())
            w1c = [sb(f"w1c{i}", [128, D], F32, p1_stack) for i in range(2)]
            w1tc = [sb(f"w1tc{i}", [128, 2048], F32, p1_stack) for i in range(2)]
            a1raw = sb("a1raw", [128, HC], F32, p1_stack)
            p1_state = {"w1rel": [None, None], "sg1rel": [None, None], "red1": []}

            def w1a_load(hc, st=p1_state):
                i = hc % 2
                src = t_w1n.ap()[hc * 128:(hc + 1) * 128, :]
                p.emit("gp", lambda e, i=i, s=src: e.dma_start(
                    out=w1c[i][:, :], in_=s), deps=[st["w1rel"][0]], dma=True,
                    sem=f"w1a{i}")

            def w1_alpha(hc, st=p1_state):
                if hc == 0:
                    w1a_load(0)
                if hc + 1 < HC:
                    w1a_load(hc + 1)
                i = hc % 2
                rd = p.emit("dve", lambda e, i=i, hc=hc: e.reduce_sum(
                    a1raw[:, hc:hc + 1], w1c[i][:, :], mybir.AxisListType.X,
                    apply_absolute_value=True), deps=[p.last(f"w1a{i}")])
                st["w1rel"][0] = rd
                st["red1"].append(rd)

            def w1g_load(k, st=p1_state):
                c, half = k // 2, k % 2
                i = k % 2
                src = t_w1t.ap()[c * 128:(c + 1) * 128,
                                 half * 2048:(half + 1) * 2048]
                p.emit("gp", lambda e, i=i, s=src: e.dma_start(
                    out=w1tc[i][:, :], in_=s), deps=[st["sg1rel"][0]], dma=True,
                    sem=f"w1g{i}")

            def w1_sign(k, st=p1_state):
                if k == 0:
                    w1g_load(0)
                if k + 1 < 2 * DC:
                    w1g_load(k + 1)
                c, half = k // 2, k % 2
                i = k % 2
                sg = p.emit("act", lambda e, i=i, c=c, half=half: e.activation(
                    out=sw1T[:, c * H + half * 2048: c * H + (half + 1) * 2048],
                    in_=w1tc[i][:, :], func=AF.Sign), deps=[p.last(f"w1g{i}")])
                st["sg1rel"][0] = sg
                st["sgtok", k] = sg

            def alpha1_scale(st=p1_state):
                p.emit("dve", lambda e: e.tensor_single_scalar(
                    alpha1[:, :], a1raw[:, :], 1.0 / D, ALU.mult),
                    deps=st["red1"][-1:])

            p1_units = []
            for hc in range(HC):
                p1_units.append(lambda hc=hc: w1_alpha(hc))
                if hc % 2 == 0:
                    p1_units.append(lambda k=hc // 2: w1_sign(k))
            p1_units.append(alpha1_scale)
        else:
            p1_units = []

        if upto >= 0:
            ln_pass(t_xT, sink_dram=t_xnsp, units=p1_units if upto >= 1 else None)
        elif upto >= 1:
            for u in p1_units:
                u()
            p.barrier()
            p.flush()
        if upto >= 1:
            p1_stack.close()

        w2sg_last = None
        alpha2_tok = None
        if upto >= 1:
            sw2T = sb("sw2T", [128, HC * D], BF16, s12)
            w2c = [sb(f"w2c{i}", [128, H], BF16, s12) for i in range(2)]
            w2tc = [sb(f"w2tc{i}", [128, D], F32, s12) for i in range(2)]
            a2raw = sb("a2raw", [128, DC], F32, s12)
            w2rel = [None, None]
            sg2rel = [None, None]
            red2 = None
            def w2a_load(dc):
                i = dc % 2
                src = t_w2n.ap()[dc * 128:(dc + 1) * 128, :]
                p.emit("gp", lambda e, i=i, s=src: e.dma_start(
                    out=w2c[i][:, :], in_=s), deps=[w2rel[0]], dma=True,
                    sem=f"w2a{i}")

            w2a_load(0)
            for dc in range(DC):
                if dc + 1 < DC:
                    w2a_load(dc + 1)
                i = dc % 2
                red2 = p.emit("dve", lambda e, i=i, dc=dc: e.reduce_sum(
                    a2raw[:, dc:dc + 1], w2c[i][:, :], mybir.AxisListType.X,
                    apply_absolute_value=True), deps=[p.last(f"w2a{i}")])
                w2rel[0] = red2
            alpha2_tok = p.emit("dve", lambda e: e.tensor_single_scalar(
                alpha2[:, :], a2raw[:, :], 1.0 / H, ALU.mult), deps=[red2])

            def w2g_load(hc):
                i = hc % 2
                src = t_w2t.ap()[hc * 128:(hc + 1) * 128, :]
                p.emit("sp", lambda e, i=i, s=src: e.dma_start(
                    out=w2tc[i][:, :], in_=s), deps=[sg2rel[0]], dma=True,
                    sem=f"w2g{i}")

            def w2_sign(hc):
                if hc == 0:
                    w2g_load(0)
                if hc + 1 < HC:
                    w2g_load(hc + 1)
                i = hc % 2
                sg = p.emit("act", lambda e, i=i, hc=hc: e.activation(
                    out=sw2T[:, hc * D:(hc + 1) * D],
                    in_=w2tc[i][:, :], func=AF.Sign), deps=[p.last(f"w2g{i}")])
                sg2rel[0] = sg

            w2units = [lambda hc=hc: w2_sign(hc) for hc in range(HC)]

        if upto >= 2:
            with ExitStack() as st:
                xnck = [sb(f"f_xn{i}", [128, DC * 512], BF16, st) for i in range(2)]
                gTc = sb("gTc", [128, HC * 512], BF16, st)
                xrs = [sb(f"xrs{i}", [128, 512], F32, st) for i in range(2)]
                sc1 = sb("sc1", [128, 512], BF16, st)
                gfl = sb("gfl", [1, 1], BF16, st)
                pout = [sb("pout0", [128, 512], BF16, st)] * 2
                xn_rel = [None, None]
                xr_rel = [p.last("w2a0"), p.last("w2a1")]
                po_rel = [None, None]
                prev_B_last = None
                for tc in range(TC):
                    i = tc % 2
                    src = t_xnsp.ap()[:, tc * DC * 512:(tc + 1) * DC * 512]
                    p.emit("sp", lambda e, i=i, s=src: e.dma_start(
                        out=xnck[i][:, :], in_=s), deps=[xn_rel[i]], dma=True,
                        sem=f"p2x{i}")
                    ld = p.last(f"p2x{i}")
                    gel_tok = {}
                    for h in range(HC):
                        bank = h % 4
                        dd = p.bank_take(bank)
                        mmt = None
                        for c in range(DC):
                            mmt = p.emit("pe", lambda e, h=h, c=c, i=i, bank=bank: e.matmul(
                                pb[bank][:, :],
                                sw1T[:, c * H + h * 128: c * H + (h + 1) * 128],
                                xnck[i][:, c * 512:(c + 1) * 512],
                                start=(c == 0), stop=(c == DC - 1)),
                                deps=[ld] + dd, inc=(c == DC - 1))
                        gel_tok[h] = p.bank_read(bank, p.emit(
                            "act", lambda e, h=h, bank=bank: e.activation(
                                out=gTc[:, h * 512:(h + 1) * 512], in_=pb[bank][:, :],
                                func=AF.Gelu, scale=alpha1[:, h:h + 1],
                                bias=b1v[:, h:h + 1]),
                            deps=[mmt, prev_B_last]))
                        if w2units:
                            w2units.pop(0)()
                    xn_rel[i] = gel_tok[HC - 1]
                    gflush = p.emit("act", lambda e: e.activation(
                        out=gfl[0:1, 0:1], in_=gTc[0:1, 0:1], func=AF.Identity),
                        deps=[gel_tok[HC - 1]])
                    for half in range(2):
                        for dq4 in range(4):
                            dout = half * 4 + dq4
                            bank = 4 + dq4
                            dd = p.bank_take(bank)
                            mmt = None
                            for h in range(HC):
                                mmt = p.emit("pe", lambda e, h=h, dout=dout, bank=bank: e.matmul(
                                    pb[bank][:, :],
                                    sw2T[:, h * D + dout * 128: h * D + (dout + 1) * 128],
                                    gTc[:, h * 512:(h + 1) * 512],
                                    start=(h == 0), stop=(h == HC - 1)),
                                    deps=[gel_tok[h], gflush, sg2rel[0], sg2rel[1]]
                                    + (dd if h == 0 else []),
                                    inc=(h == HC - 1))
                            j = dout % 2
                            xsrc = t_xT.ap()[:, tc * DC * 512 + dout * 512:
                                             tc * DC * 512 + (dout + 1) * 512]
                            p.emit("gp", lambda e, j=j, s=xsrc: e.dma_start(
                                out=xrs[j][:, :], in_=s), deps=[xr_rel[j]],
                                dma=True, sem=f"p2r{j}")
                            lt = p.last(f"p2r{j}")
                            et = p.bank_read(bank, p.emit(
                                "act", lambda e, dout=dout, bank=bank: e.activation(
                                    out=sc1[:, :], in_=pb[bank][:, :], func=AF.Identity,
                                    scale=alpha2[:, dout:dout + 1],
                                    bias=b2v[:, dout:dout + 1]),
                                deps=[mmt, prev_B_last, alpha2_tok]))
                            at = p.emit("dve", lambda e, j=j: e.tensor_tensor(
                                pout[j][:, :], sc1[:, :], xrs[j][:, :], ALU.add),
                                deps=[et, lt, po_rel[0], po_rel[1]])
                            prev_B_last = at
                            xr_rel[j] = at
                            dstp = t_pT.ap()[dout * 128:(dout + 1) * 128,
                                             tc * 512:(tc + 1) * 512]
                            po_rel[j] = p.emit("sp", lambda e, j=j, d=dstp: e.dma_start(
                                out=d, in_=pout[j][:, :]), deps=[at], dma=True,
                                sem="p2o")
                p.barrier()
                p.flush()
        if upto >= 2:
            s12.close()

        if upto >= 3:
            s35 = ctx.enter_context(ExitStack())
            fT = sb("fT", [128, DC * T], BF16, s35)
            negM = sb("negM", [1, T], BF16, s35)
            with ExitStack() as st:
                W0b = sb("W0b", [128, DC * D], BF16, st)
                xnck = [sb(f"p3xn{i}", [128, DC * 512], BF16, st) for i in range(2)]
                w0src = hap(t_W0, 0, [[D, 128], [128 * D, DC], [1, D]])
                w0t = p.emit("gp", lambda e: e.dma_start(out=W0b[:, :], in_=w0src),
                             dma=True, sem="p3w")
                xn_rel = [None, None]
                for tc in range(TC):
                    i = tc % 2
                    src = t_xnsp.ap()[:, tc * DC * 512:(tc + 1) * DC * 512]
                    p.emit("sp", lambda e, i=i, s=src: e.dma_start(
                        out=xnck[i][:, :], in_=s), deps=[xn_rel[i]], dma=True,
                        sem=f"p3x{i}")
                    ld = p.last(f"p3x{i}")
                    last_ev = None
                    for dout in range(DC):
                        dd = p.bank_take(dout)
                        mmt = None
                        for c in range(DC):
                            mmt = p.emit("pe", lambda e, c=c, dout=dout, i=i: e.matmul(
                                pb[dout][:, :],
                                W0b[:, c * D + dout * 128: c * D + (dout + 1) * 128],
                                xnck[i][:, c * 512:(c + 1) * 512],
                                start=(c == 0), stop=(c == DC - 1)),
                                deps=[ld, w0t] + dd, inc=(c == DC - 1))
                        last_ev = p.bank_read(dout, p.emit(
                            "act", lambda e, dout=dout, tc=tc: e.activation(
                                out=fT[:, dout * T + tc * 512: dout * T + (tc + 1) * 512],
                                in_=pb[dout][:, :], func=AF.Identity,
                                bias=b0v[:, dout:dout + 1]), deps=[mmt]))
                    xn_rel[i] = last_ev
                p.barrier()
                p.flush()

        if upto >= 4:
            f_tok = sb("f_tok", [128, TT * D], BF16, s35)
            band = sb("band", [128, 1152], F32, s35)
            with ExitStack() as st:
                sq = [sb(f"sqf{i}", [128, T], BF16, st) for i in range(2)]
                tk = p.emit("gp", lambda e: e.iota(band[:, :], [[-1, 1152]], base=512,
                                                   channel_multiplier=1,
                                                   allow_small_or_imprecise_dtypes=True))
                tk = p.emit("act", lambda e: e.activation(
                    out=band[:, :], in_=band[:, :], func=AF.Abs), deps=[tk])
                tk = p.emit("dve", lambda e: e.tensor_single_scalar(
                    band[:, :], band[:, :], 10.0, ALU.min), deps=[tk])
                p.emit("dve", lambda e: e.tensor_scalar(
                    band[:, :], band[:, :], 0.01, -0.1, ALU.mult, ALU.add), deps=[tk])
                for c in range(DC):
                    dst = bass.AP(f_tok, c * 128, [[TT * D, 128], [D, TT], [1, 128]])
                    p.emit("sp", lambda e, d=dst, c=c: e.dma_start_transpose(
                        d, fT[:, c * T:(c + 1) * T]), dma=True, sem="p4t")
                mm_last = {}
                prev_sq = None
                for c in range(DC):
                    i = c % 2
                    sqt = p.emit("act", lambda e, c=c, i=i: e.activation(
                        out=sq[i][:, :], in_=fT[:, c * T:(c + 1) * T], func=AF.Square),
                        deps=[prev_sq])
                    for tc in range(TC):
                        dd = p.bank_take(tc) if c == 0 else []
                        tok = p.emit("pe", lambda e, c=c, i=i, tc=tc: e.matmul(
                            pb[tc][0:1, :], ones_col[:, :],
                            sq[i][:, tc * 512:(tc + 1) * 512],
                            start=(c == 0), stop=(c == DC - 1)),
                            deps=[sqt] + dd,
                            inc=(c == DC - 1) or (tc == TC - 1))
                        if c == DC - 1:
                            mm_last[tc] = tok
                    prev_sq = tok
                for tc in range(TC):
                    p.bank_read(tc, p.emit("dve", lambda e, tc=tc: e.tensor_single_scalar(
                        negM[0:1, tc * 512:(tc + 1) * 512], pb[tc][0:1, :], -1.0,
                        ALU.mult), deps=[mm_last[tc]]))
                p.barrier()
                p.flush()

        if upto >= 5:
            with ExitStack() as st:
                at = sb("atil", [128, TT * 512], BF16, st)
                zinv = sb("zinv", [1, 512], F32, st)
                flsc = sb("flsc", [1, 1], F32, st)
                zinvB = sb("zinvB", [128, 512], F32, st)
                negMB = sb("negMB", [128, 512], F32, st)
                pc = [sb(f"pchunk{i}", [128, 512], BF16, st) for i in range(2)]
                sc5 = [sb(f"sc5{i}", [128, 512], F32, st) for i in range(2)]
                x1o = [sb(f"x1o{i}", [128, 512], F32, st) for i in range(2)]
                pc_rel = [None, None]
                x1o_rel = [None, None]
                bandtok = None
                eb = [0, 1, 4, 5]
                nb_prev = None
                for g in range(TC):
                    exp_tok = {}
                    z_tok = None
                    dz = p.bank_take(2)
                    d3b = p.bank_take(3)
                    nb_mm = p.emit("pe", lambda e, g=g: e.matmul(
                        pb[3][:, :], ones_row[0:1, :],
                        negM[0:1, g * 512:(g + 1) * 512],
                        start=True, stop=True), deps=d3b)
                    nb_tok = p.bank_read(3, p.emit(
                        "dve", lambda e: e.tensor_copy(negMB[:, :], pb[3][:, :]),
                        deps=[nb_mm, nb_prev]))

                    def emit_z(tt, g=g, dz=dz):
                        return p.emit("pe", lambda e, tt=tt: e.matmul(
                            pb[2][0:1, :], ones_col[:, :],
                            at[:, tt * 512:(tt + 1) * 512],
                            start=(tt == 0), stop=(tt == TT - 1)),
                            deps=[exp_tok[tt]] + (dz if tt == 0 else []),
                            inc=(tt == TT - 1))

                    for tt in range(TT):
                        bank = eb[tt % 4]
                        dd = p.bank_take(bank)
                        mmt = None
                        for c in range(DC):
                            mmt = p.emit("pe", lambda e, c=c, tt=tt, g=g, bank=bank: e.matmul(
                                pb[bank][:, :],
                                fT[:, c * T + tt * 128: c * T + (tt + 1) * 128],
                                fT[:, c * T + g * 512: c * T + (g + 1) * 512],
                                start=(c == 0), stop=(c == DC - 1)),
                                deps=dd if c == 0 else [], inc=(c == DC - 1))
                        madd = p.emit("dve", lambda e, bank=bank: e.tensor_tensor(
                            pb[bank][:, :], pb[bank][:, :], negMB[:, :],
                            ALU.add), deps=[mmt, nb_tok])
                        nb_prev = madd
                        delta = 128 * tt - 512 * g
                        btok = None
                        if -128 <= delta <= 512:
                            btok = p.emit("dve", lambda e, d=delta, bank=bank: e.tensor_tensor(
                                pb[bank][:, :], pb[bank][:, :], band[:, 512 - d:1024 - d],
                                ALU.add), deps=[madd, bandtok])
                        et = p.bank_read(bank, p.emit(
                            "act", lambda e, tt=tt, bank=bank: e.activation(
                                out=at[:, tt * 512:(tt + 1) * 512], in_=pb[bank][:, :],
                                func=AF.Exp), deps=[btok if btok else madd]))
                        exp_tok[tt] = et
                        if tt >= 1:
                            z_tok = emit_z(tt - 1)
                    flush = p.emit("act", lambda e: e.activation(
                        out=flsc[0:1, 0:1], in_=at[0:1, 0:1], func=AF.Identity),
                        deps=[exp_tok[TT - 1]])
                    z_tok = p.emit("pe", lambda e: e.matmul(
                        pb[2][0:1, :], ones_col[:, :],
                        at[:, (TT - 1) * 512: TT * 512],
                        start=False, stop=True),
                        deps=[exp_tok[TT - 1], flush], inc=True)
                    zt = p.bank_read(2, p.emit(
                        "dve", lambda e: e.reciprocal(zinv[0:1, :], pb[2][0:1, :]),
                        deps=[z_tok]))
                    d3 = p.bank_take(3)
                    zb = p.emit("pe", lambda e: e.matmul(
                        pb[3][:, :], ones_rf[0:1, :], zinv[0:1, :],
                        start=True, stop=True), deps=[zt] + d3)
                    zc = p.bank_read(3, p.emit(
                        "dve", lambda e: e.tensor_copy(zinvB[:, :], pb[3][:, :]),
                        deps=[zb]))
                    for dout in range(DC):
                        bank = 4 + dout % 4
                        dd = p.bank_take(bank)
                        mmt = None
                        for tt in range(TT):
                            mmt = p.emit("pe", lambda e, tt=tt, dout=dout, bank=bank: e.matmul(
                                pb[bank][:, :],
                                f_tok[:, tt * D + dout * 128: tt * D + (dout + 1) * 128],
                                at[:, tt * 512:(tt + 1) * 512],
                                start=(tt == 0), stop=(tt == TT - 1)),
                                deps=[exp_tok[tt]] + (dd if tt == 0 else []),
                                inc=(tt == TT - 1))
                        j = dout % 2
                        psrc = t_pT.ap()[dout * 128:(dout + 1) * 128,
                                         g * 512:(g + 1) * 512]
                        p.emit("gp", lambda e, j=j, s=psrc: e.dma_start(
                            out=pc[j][:, :], in_=s), deps=[pc_rel[j]], dma=True,
                            sem=f"p5p{j}")
                        m1 = p.bank_read(bank, p.emit(
                            "dve", lambda e, j=j, bank=bank: e.tensor_tensor(
                                sc5[j][:, :], pb[bank][:, :], zinvB[:, :], ALU.mult),
                            deps=[mmt, zc]))
                        m2 = p.emit("dve", lambda e, j=j: e.tensor_tensor(
                            x1o[j][:, :], sc5[j][:, :], pc[j][:, :], ALU.add),
                            deps=[m1, p.last(f"p5p{j}"), x1o_rel[j]])
                        pc_rel[j] = m2
                        dst = t_x1T.ap()[:, g * DC * 512 + dout * 512:
                                         g * DC * 512 + (dout + 1) * 512]
                        x1o_rel[j] = p.emit("sp", lambda e, j=j, d=dst: e.dma_start(
                            out=d, in_=x1o[j][:, :]), deps=[m2], dma=True,
                            sem=f"p5o{j}")
                    p.barrier()
                p.flush()
        if upto >= 5 and s35 is not None:
            s35.close()

        if upto >= 6:
            s68 = ctx.enter_context(ExitStack())
            xn1T = sb("xn1T", [128, DC * T], BF16, s68)
            s67 = ctx.enter_context(ExitStack())
            binb = sb("binb", [1, WID], BF16, s67)
            p.emit("gp", lambda e: e.dma_start(out=binb[:, :], in_=t_binr.ap()),
                   dma=True, sem="p6b")
            ln_pass(t_x1T, sink_sbuf=xn1T)

        if upto >= 7:
            with ExitStack() as st:
                WinTb = sb("WinTb", [128, DC * WID], BF16, st)
                WoutTb = sb("WoutTb", [128, WK * D], BF16, st)
                silu_t = [sb(f"silu_t{i}", [128, WID], BF16, st) for i in range(2)]
                dec_t = [sb(f"dec_t{i}", [128, WID], BF16, st) for i in range(2)]
                mask_t = [sb(f"mask_t{i}", [128, WID], BF16, st) for i in range(2)]
                masked_t = [sb(f"masked_t{i}", [128, WID], BF16, st) for i in range(2)]
                mTc = [sb(f"mTc{i}", [128, WK * 512], BF16, st) for i in range(2)]
                xrs = [sb(f"p7xr{i}", [128, 512], F32, st) for i in range(2)]
                p1o = [sb(f"p1o{i}", [128, 512], BF16, st) for i in range(2)]
                wis = hap(t_WinT, 0, [[WID, 128], [128 * WID, DC], [1, WID]])
                p.emit("gp", lambda e: e.dma_start(out=WinTb[:, :], in_=wis),
                       dma=True, sem="p7w")
                wos = hap(t_WoutT, 0, [[D, 128], [128 * D, WK], [1, D]])
                p.emit("gp", lambda e: e.dma_start(out=WoutTb[:, :], in_=wos),
                       dma=True, sem="p7w")
                wt1 = wt2 = p.last("p7w")
                xr_rel = [wt1, wt1]
                po_rel = [None, None]
                mtc_rel = [None, None]
                tr_all = [None] * TC
                route_rel = [None, None]

                prev_tr = [None, None]

                def stage_a_tt(tcc, ii):
                    i = tcc % 2
                    if True:
                        tt = tcc * 4 + ii
                        i2 = tt % 2
                        sd_toks = []
                        for wc in range(WC):
                            bank = wc % 4
                            dd = p.bank_take(bank)
                            for c in range(DC):
                                p.emit("pe", lambda e, c=c, tt=tt, wc=wc, bank=bank: e.matmul(
                                    pb[bank][:, :],
                                    xn1T[:, c * T + tt * 128: c * T + (tt + 1) * 128],
                                    WinTb[:, c * WID + wc * 512: c * WID + (wc + 1) * 512],
                                    start=(c == 0), stop=False),
                                    deps=[wt1] + (dd if c == 0 else []), inc=False)
                            mmt = p.emit("pe", lambda e, wc=wc, bank=bank: e.matmul(
                                pb[bank][:, :], ones_row[0:1, :],
                                binb[0:1, wc * 512:(wc + 1) * 512],
                                start=False, stop=True), inc=True)
                            s1 = p.bank_read(bank, p.emit(
                                "act", lambda e, wc=wc, i2=i2, bank=bank: e.activation(
                                    out=silu_t[i2][:, wc * 512:(wc + 1) * 512],
                                    in_=pb[bank][:, :], func=AF.Silu),
                                deps=[mmt, route_rel[i2]]))
                            s2 = p.bank_read(bank, p.emit(
                                "dve", lambda e, wc=wc, i2=i2, bank=bank: e.tensor_single_scalar(
                                    dec_t[i2][:, wc * 512:(wc + 1) * 512], pb[bank][:, :],
                                    0.0, ALU.is_gt), deps=[mmt, route_rel[i2], s1]))
                            sd_toks += [s1, s2]
                        mt = p.emit("dve", lambda e, i2=i2: e.memset(
                            bass.AP(mask_t[i2], NNODE, [[WID, 128], [256, NTREE], [1, 1]]),
                            0.0), deps=sd_toks + [prev_tr[i2]])
                        mt = p.emit("dve", lambda e, i2=i2: e.memset(
                            bass.AP(mask_t[i2], 0, [[WID, 128], [256, NTREE], [1, 1]]),
                            1.0), deps=[mt])
                        for lev in range(7):
                            o = (1 << lev) - 1
                            o2 = (1 << (lev + 1)) - 1
                            cnt = 1 << lev
                            pv = bass.AP(mask_t[i2], o, [[WID, 128], [256, NTREE], [1, cnt]])
                            dv = bass.AP(dec_t[i2], o, [[WID, 128], [256, NTREE], [1, cnt]])
                            ov = bass.AP(mask_t[i2], o2 + cnt,
                                         [[WID, 128], [256, NTREE], [1, cnt]])
                            ev_ = bass.AP(mask_t[i2], o2,
                                          [[WID, 128], [256, NTREE], [1, cnt]])
                            mt = p.emit("dve", lambda e, pv=pv, dv=dv, ov=ov: e.tensor_tensor(
                                ov, pv, dv, ALU.mult), deps=[mt])
                            mt = p.emit("dve", lambda e, pv=pv, ov=ov, ev_=ev_: e.tensor_tensor(
                                ev_, pv, ov, ALU.subtract), deps=[mt])
                        mk = p.emit("dve", lambda e, i2=i2: e.tensor_tensor(
                            masked_t[i2][:, :], silu_t[i2][:, :], mask_t[i2][:, :],
                            ALU.mult), deps=[mt])
                        route_rel[i2] = mk
                        dstm = bass.AP(mTc[i], ii * 128,
                                       [[WK * 512, 128], [512, WK], [1, 128]])
                        trk = p.emit("sp", lambda e, d=dstm, i=i, i2=i2: e.dma_start_transpose(
                            d, masked_t[i2][:, :]),
                            deps=[mk, mtc_rel[i]] if ii == 0 else [mk], dma=True,
                            sem=f"p7t{i2}")
                        prev_tr[i2] = trk
                        if ii == 3:
                            tr_all[tcc] = [prev_tr[0], prev_tr[1]]

                def stage_b_d2(tcc, k):
                    i = tcc % 2
                    for dout in (2 * k, 2 * k + 1):
                        bank = 4 + dout % 4
                        dd = p.bank_take(bank)
                        mmt = None
                        for wc in range(WK):
                            mmt = p.emit("pe", lambda e, wc=wc, dout=dout, i=i, bank=bank: e.matmul(
                                pb[bank][:, :],
                                WoutTb[:, wc * D + dout * 128: wc * D + (dout + 1) * 128],
                                mTc[i][:, wc * 512:(wc + 1) * 512],
                                start=(wc == 0), stop=(wc == WK - 1)),
                                deps=[wt2] + tr_all[tcc] + (dd if wc == 0 else []),
                                inc=(wc == WK - 1))
                        j = dout % 2
                        xsrc = t_x1T.ap()[:, tcc * DC * 512 + dout * 512:
                                          tcc * DC * 512 + (dout + 1) * 512]
                        p.emit("gp", lambda e, j=j, s=xsrc: e.dma_start(
                            out=xrs[j][:, :], in_=s), deps=[xr_rel[j]], dma=True,
                            sem=f"p7r{j}")
                        at2 = p.bank_read(bank, p.emit(
                            "dve", lambda e, j=j, bank=bank: e.tensor_tensor(
                                p1o[j][:, :], pb[bank][:, :], xrs[j][:, :], ALU.add),
                            deps=[mmt, p.last(f"p7r{j}"), po_rel[j]]))
                        xr_rel[j] = at2
                        dst = t_p1T.ap()[dout * 128:(dout + 1) * 128,
                                         tcc * 512:(tcc + 1) * 512]
                        po_rel[j] = p.emit("sp", lambda e, j=j, d=dst: e.dma_start(
                            out=d, in_=p1o[j][:, :]), deps=[at2], dma=True,
                            sem=f"p7o{j}")
                        if dout == DC - 1:
                            mtc_rel[i] = mmt

                for tcc in range(TC):
                    for ii in range(4):
                        stage_a_tt(tcc, ii)
                        if tcc >= 1:
                            stage_b_d2(tcc - 1, ii)
                for k in range(4):
                    stage_b_d2(TC - 1, k)
                p.barrier()
                p.flush()
        if upto >= 7:
            s67.close()

        if upto >= 8:
            with ExitStack() as st:
                WvTb = sb("WvTb", [128, DC * D], BF16, st)
                WoTb = sb("WoTb", [128, DC * D], BF16, st)
                vsub = sb("vsub", [128, 1024], F32, st)
                fsc = sb("fsc", [128, DC], F32, st)
                zofft = sb("zofft", [128, 4 * DC], F32, st)
                cstk = sb("cstk", [128, 4], F32, st)
                carr = sb("carr", [128, 2 * DC], F32, st)
                tr = sb("tramp", [128, 1024], F32, st)
                s_q = [sb(f"s_q{i}", [128, 1024], F32, st) for i in range(2)]
                s_n = [sb(f"s_n{i}", [128, 1024], F32, st) for i in range(2)]
                s_f = [sb(f"s_f{i}", [128, 1024], F32, st) for i in range(2)]
                s_sin = [sb(f"s_sin{i}", [128, 1024], F32, st) for i in range(2)]
                s_cos = [sb(f"s_cos{i}", [128, 1024], F32, st) for i in range(2)]
                cA = [sb(f"cA{i}", [128, 1024], F32, st) for i in range(2)]
                cB = [sb(f"cB{i}", [128, 1024], F32, st) for i in range(2)]
                rp_a = sb("rp_a", [128, 1024], BF16, st)
                rp_b = sb("rp_b", [128, 1024], BF16, st)
                ret8 = [[sb(f"ret8_{pp}_{j}", [128, 1024], BF16, st)
                         for j in range(DC)] for pp in range(2)]
                p1c = [sb(f"p1c8_{i}", [128, 512], BF16, st) for i in range(4)]
                osb = [sb(f"osb8_{i}", [128, 512], F32, st) for i in range(2)]
                onesT = sb("onesT", [128, 1024], BF16, st)
                ot = p.emit("dve", lambda e: e.memset(onesT[:, :], 1.0))
                p.emit("dve", lambda e: e.memset(cstk[:, 0:1], MAGIC))
                p.emit("dve", lambda e: e.memset(cstk[:, 1:2], -MAGIC))
                ck = p.emit("dve", lambda e: e.memset(cstk[:, 2:3], 0.25))
                fk = p.emit("dve", lambda e: e.tensor_single_scalar(
                    fsc[:, :], freqv[:, :], INV2PI, ALU.mult))
                zkt = []
                for z in range(4):
                    zkt.append(p.emit("dve", lambda e, z=z: e.tensor_single_scalar(
                        zofft[:, z * DC:(z + 1) * DC], fsc[:, :],
                        1024.0 * z, ALU.mult), deps=[fk]))
                wvl = hap(t_WvT, 0, [[D, 128], [128 * D, DC], [1, D]])
                p.emit("gp", lambda e: e.dma_start(out=WvTb[:, :], in_=wvl),
                       dma=True, sem="p8w")
                wol = hap(t_WoT, 0, [[D, 128], [128 * D, DC], [1, D]])
                p.emit("gp", lambda e: e.dma_start(out=WoTb[:, :], in_=wol),
                       dma=True, sem="p8w")
                trt = p.emit("gp", lambda e: e.iota(
                    tr[:, :], [[1, 1024]], base=0, channel_multiplier=0,
                    allow_small_or_imprecise_dtypes=True))
                wvt = wot = p.last("p8w")
                prev = {}
                ret_tok = [[None] * DC, [None] * DC]
                ret_rel = [None, None]
                pend_ret = [None]
                p1_rel = [None] * 4
                os_rel = [None, None]
                pk = [0, 0]
                kk_ = [0]

                st8 = {}

                def chainA(ts_, j):
                    g = kk_[0] % 2
                    kk_[0] += 1
                    mm_v = []
                    for h2 in range(2):
                        tc = ts_ * 2 + h2
                        bank = kk_[0] % 2 * 2 + h2
                        dd = p.bank_take(bank)
                        mmt = None
                        for c in range(DC):
                            mmt = p.emit("pe", lambda e, c=c, j=j, tc=tc, bank=bank: e.matmul(
                                pb[bank][:, :],
                                WvTb[:, c * D + j * 128: c * D + (j + 1) * 128],
                                xn1T[:, c * T + tc * 512: c * T + (tc + 1) * 512],
                                start=(c == 0), stop=(c == DC - 1)),
                                deps=[wvt] + (dd if c == 0 else []),
                                inc=(c == DC - 1))
                        mm_v.append((bank, mmt))
                    t1 = p.emit("act", lambda e, g=g, j=j, z=ts_: e.activation(
                        out=s_q[g][:, :], in_=tr[:, :], func=AF.Identity,
                        scale=fsc[:, j:j + 1],
                        bias=zofft[:, z * DC + j: z * DC + j + 1]),
                        deps=[trt, fk, zkt[ts_], prev.get(("q", g))])
                    t2a = p.emit("act", lambda e, g=g: e.activation(
                        out=s_n[g][:, :], in_=s_q[g][:, :], func=AF.Identity,
                        bias=cstk[:, 0:1]), deps=[t1, ck, prev.get(("n", g))])
                    t2b = p.emit("act", lambda e, g=g: e.activation(
                        out=s_n[g][:, :], in_=s_n[g][:, :], func=AF.Identity,
                        bias=cstk[:, 1:2]), deps=[t2a])
                    t3 = p.emit("dve", lambda e, g=g: e.tensor_tensor(
                        s_f[g][:, :], s_q[g][:, :], s_n[g][:, :], ALU.subtract),
                        deps=[t2b, prev.get(("f", g))])
                    st8[(ts_, j)] = (g, mm_v, t3)

                def chainB(ts_, j):
                    g, mm_v, t3 = st8.pop((ts_, j))
                    rp = ts_ % 2
                    if pend_ret[0] is not None:
                        fn = pend_ret[0]
                        pend_ret[0] = None
                        fn()
                    evs = []
                    for h2, (bank, mmt) in enumerate(mm_v):
                        evs.append(p.bank_read(bank, p.emit(
                            "act", lambda e, h2=h2, bank=bank: e.activation(
                                out=vsub[:, h2 * 512:(h2 + 1) * 512],
                                in_=pb[bank][:, :], func=AF.Identity),
                            deps=[mmt, prev.get("vsub")])))
                    sn = p.emit("act", lambda e, g=g: e.activation(
                        out=s_sin[g][:, :], in_=s_f[g][:, :], func=AF.Sin,
                        scale=TWOPI), deps=[t3, prev.get(("sin", g))])
                    t4 = p.emit("act", lambda e, g=g: e.activation(
                        out=s_q[g][:, :], in_=s_f[g][:, :], func=AF.Identity,
                        bias=cstk[:, 2:3]), deps=[t3])
                    t5a = p.emit("act", lambda e, g=g: e.activation(
                        out=s_n[g][:, :], in_=s_q[g][:, :], func=AF.Identity,
                        bias=cstk[:, 0:1]), deps=[t4, t3])
                    t5b = p.emit("act", lambda e, g=g: e.activation(
                        out=s_n[g][:, :], in_=s_n[g][:, :], func=AF.Identity,
                        bias=cstk[:, 1:2]), deps=[t5a])
                    t6 = p.emit("dve", lambda e, g=g: e.tensor_tensor(
                        s_f[g][:, :], s_q[g][:, :], s_n[g][:, :], ALU.subtract),
                        deps=[t5b, sn])
                    cs = p.emit("act", lambda e, g=g: e.activation(
                        out=s_cos[g][:, :], in_=s_f[g][:, :], func=AF.Sin,
                        scale=TWOPI), deps=[t6, prev.get(("cos", g))])
                    vr1 = p.emit("dve", lambda e, g=g: e.tensor_tensor(
                        s_q[g][:, :], vsub[:, :], s_cos[g][:, :], ALU.mult),
                        deps=[cs, t6] + evs)
                    vr2 = p.emit("dve", lambda e, g=g: e.tensor_tensor(
                        s_n[g][:, :], vsub[:, :], s_sin[g][:, :], ALU.mult),
                        deps=[sn, t6, vr1])
                    prev["vsub"] = vr2
                    ia = 0.0 if ts_ == 0 else carr[:, 2 * j: 2 * j + 1]
                    ib = 0.0 if ts_ == 0 else carr[:, 2 * j + 1: 2 * j + 2]
                    sc_a = p.emit("dve", lambda e, k=j % 2, g=g, ia=ia: e.tensor_tensor_scan(
                        cA[k][:, :], onesT[:, :], s_q[g][:, :], ia, ALU.mult,
                        ALU.add),
                        deps=[vr1, ot, prev.get(("car", j)), prev.get(("cA", j % 2))])
                    sc_b = p.emit("dve", lambda e, k=j % 2, g=g, ib=ib: e.tensor_tensor_scan(
                        cB[k][:, :], onesT[:, :], s_n[g][:, :], ib, ALU.mult,
                        ALU.add), deps=[vr2, prev.get(("cB", j % 2))])
                    ca = p.emit("dve", lambda e, k=j % 2, j=j: e.tensor_copy(
                        carr[:, 2 * j: 2 * j + 1], cA[k][:, 1023:1024]),
                        deps=[sc_a])
                    cb = p.emit("dve", lambda e, k=j % 2, j=j: e.tensor_copy(
                        carr[:, 2 * j + 1: 2 * j + 2], cB[k][:, 1023:1024]),
                        deps=[sc_b])
                    prev[("car", j)] = cb
                    r1 = p.emit("gp", lambda e, k=j % 2, g=g: e.tensor_tensor(
                        rp_a[:, :], s_cos[g][:, :], cA[k][:, :], ALU.mult),
                        deps=[sc_a, ca, prev.get("rp")])
                    r2 = p.emit("gp", lambda e, k=j % 2, g=g: e.tensor_tensor(
                        rp_b[:, :], s_sin[g][:, :], cB[k][:, :], ALU.mult),
                        deps=[sc_b, cb])

                    def emit_ret(rp=rp, j=j, r1=r1, r2=r2):
                        ret = p.emit("dve", lambda e, rp=rp, j=j: e.tensor_tensor(
                            ret8[rp][j][:, :], rp_a[:, :], rp_b[:, :], ALU.add),
                            deps=[r1, r2, ret_rel[rp]])
                        ret_tok[rp][j] = ret
                        prev["rp"] = ret
                    pend_ret[0] = emit_ret
                    prev[("q", g)] = sc_a
                    prev[("n", g)] = sc_b
                    prev[("f", g)] = cs
                    prev[("sin", g)] = r2
                    prev[("cos", g)] = r1
                    prev[("cA", j % 2)] = r1
                    prev[("cB", j % 2)] = r2

                def wave(ts_, wv):
                    rp = ts_ % 2
                    half, grp = wv // 2, wv % 2
                    co = ts_ * 1024 + half * 512
                    lts = []
                    for dd_ in range(4):
                        dout = grp * 4 + dd_
                        i4 = pk[0] % 4
                        pk[0] += 1
                        psrc = t_p1T.ap()[dout * 128:(dout + 1) * 128, co:co + 512]
                        lts.append((i4, p.emit("sp", lambda e, i4=i4, s=psrc: e.dma_start(
                            out=p1c[i4][:, :], in_=s), deps=[p1_rel[i4]],
                            dma=True, sem=f"p8p{i4}")))
                    mms = []
                    for dd_ in range(4):
                        dout = grp * 4 + dd_
                        bank = 4 + dd_
                        dbt = p.bank_take(bank)
                        mmt = None
                        for j in range(DC):
                            mmt = p.emit("pe", lambda e, j=j, dout=dout, half=half, bank=bank, rp=rp: e.matmul(
                                pb[bank][:, :],
                                WoTb[:, j * D + dout * 128: j * D + (dout + 1) * 128],
                                ret8[rp][j][:, half * 512:(half + 1) * 512],
                                start=(j == 0), stop=(j == DC - 1)),
                                deps=[wot, ret_tok[rp][j]] + (dbt if j == 0 else []),
                                inc=(j == DC - 1))
                        mms.append((bank, mmt))
                    ld_all = [p.last(f"p8p{k}") for k in range(4)]
                    for dd_ in range(4):
                        dout = grp * 4 + dd_
                        bank, mmt = mms[dd_]
                        i4, lt = lts[dd_]
                        oi = pk[1] % 2
                        pk[1] += 1
                        at3 = p.bank_read(bank, p.emit(
                            "dve", lambda e, i4=i4, oi=oi, bank=bank: e.tensor_tensor(
                                osb[oi][:, :], pb[bank][:, :], p1c[i4][:, :],
                                ALU.add), deps=[mmt, os_rel[oi]] + ld_all))
                        p1_rel[i4] = at3
                        dst = t_out.ap()[dout * 128:(dout + 1) * 128, co:co + 512]
                        os_rel[oi] = p.emit("sp", lambda e, oi=oi, d=dst: e.dma_start(
                            out=d, in_=osb[oi][:, :]), deps=[at3], dma=True,
                            sem=f"p8o{oi}")
                    if wv == 3:
                        ret_rel[rp] = mms[-1][1]

                seq = [(ts_, j) for ts_ in range(4) for j in range(DC)]
                for idx, (ts_, j) in enumerate(seq):
                    chainA(ts_, j)
                    if idx >= 1:
                        pt, pj = seq[idx - 1]
                        chainB(pt, pj)
                        if pt >= 1 and pj % 2 == 1:
                            wave(pt - 1, pj // 2)
                chainB(3, DC - 1)
                if pend_ret[0] is not None:
                    fn = pend_ret[0]
                    pend_ret[0] = None
                    fn()
                wave(2, 3)
                for wv in range(4):
                    wave(3, wv)
                p.barrier()
                p.flush()
        if upto >= 8 and s68 is not None:
            s68.close()

        p.flush()

    return nc


def prep_inputs(inputs):
    f = np.float32
    x = np.asarray(inputs["L0_attn_x"], f)
    W0 = np.ascontiguousarray(np.asarray(inputs["L0_attn_W"], f))
    b0 = np.asarray(inputs["L0_attn_b"], f)
    w1 = np.asarray(inputs["L0_ffn_w1"], f)
    b1 = np.asarray(inputs["L0_ffn_b1"], f)
    w2 = np.asarray(inputs["L0_ffn_w2"], f)
    b2 = np.asarray(inputs["L0_ffn_b2"], f)
    Wv = np.asarray(inputs["L1_attn_Wv"], f)
    Wo = np.asarray(inputs["L1_attn_Wo"], f)
    freqs = np.asarray(inputs["L1_attn_freqs"], f)
    Win = np.asarray(inputs["L1_ffn_W_in"], f)
    bin_ = np.asarray(inputs["L1_ffn_b_in"], f)
    Wout = np.asarray(inputs["L1_ffn_W_out"], f)

    shared = {
        "W0": W0,
        "b0v": np.ascontiguousarray(b0.reshape(DC, 128).T),
        "w1n": np.ascontiguousarray(w1),
        "w1t": np.ascontiguousarray(w1.T),
        "b1v": np.ascontiguousarray(b1.reshape(HC, 128).T),
        "w2n": np.ascontiguousarray(w2),
        "w2t": np.ascontiguousarray(w2.T),
        "b2v": np.ascontiguousarray(b2.reshape(DC, 128).T),
        "WvT": np.ascontiguousarray(Wv.T),
        "WoT": np.ascontiguousarray(Wo.T),
        "freqv": np.ascontiguousarray(freqs.reshape(DC, 128).T),
    }
    WinP = np.zeros((WID, D), f)
    binP = np.zeros((1, WID), f)
    WoutP = np.zeros((D, WID), f)
    wr = Win.reshape(NTREE, NNODE, D)
    br = bin_.reshape(NTREE, NNODE)
    wo = Wout.reshape(D, NTREE, NNODE)
    node_at = np.zeros(NNODE, np.int64)
    curj = [0]
    for lev in range(7):
        nxt = [2 * j for j in curj] + [2 * j + 1 for j in curj]
        o2 = (1 << (lev + 1)) - 1
        for k, j in enumerate(nxt):
            node_at[o2 + k] = o2 + j
        curj = nxt
    for t in range(NTREE):
        WinP[t * 256: t * 256 + NNODE, :] = wr[t][node_at]
        binP[0, t * 256: t * 256 + NNODE] = br[t][node_at]
        WoutP[:, t * 256: t * 256 + NNODE] = wo[:, t, :][:, node_at]
    shared["WinT"] = np.ascontiguousarray(WinP.T)
    shared["binr"] = binP
    shared["WoutT"] = np.ascontiguousarray(WoutP.T)

    in_maps = []
    for b in range(8):
        m = dict(shared)
        m["xT"] = np.ascontiguousarray(
            x[b].reshape(TC, 512, DC, 128).transpose(3, 0, 2, 1).reshape(128, TC * DC * 512))
        in_maps.append(m)
    return in_maps


def kernel(**inputs):
    in_maps = prep_inputs(inputs)
    nc = build_program(debug=False, upto=99)
    res = run_bass_kernel_spmd(nc, in_maps, core_ids=list(range(8)))
    out = np.stack([np.asarray(r["out"], np.float32).T for r in res.results], axis=0)
    return out
